# version 08 (final)
# baseline (speedup 1.0000x reference)
import contextlib
import numpy as np

import concourse.bass as bass
import concourse.mybir as mybir
from concourse.bass_utils import run_bass_kernel_spmd

F32 = mybir.dt.float32
BF16 = mybir.dt.bfloat16
AF = mybir.ActivationFunctionType
ALU = mybir.AluOpType

B, S, D = 2, 8192, 2048
T = 2048
NT, TT = 8, 256
H, G = 16, 4
MC = D // 128
OC = D // 128
FH = 5632
FHC = FH // 128
HHALF = FHC // 2
NW = FH * D
PAIRS = NT // 2
NCH = T // 128
SW = G * 129

LN_EPS = 1e-5
ATT_EPS = 1e-6
ROPE_BASE = 10000.0

ENGS = ("sync", "gpsimd", "tensor", "vector", "scalar")
DEBUG_SCRATCH = False
_SKIP_CC = False
_SKIP_P3 = False
_SKIP_P2 = False


class Emitter:

    def __init__(self, nc, es):
        self.h = {}
        self.n = {}
        for name in ("LDX", "LDK", "LDS", "LDW0", "LDW1", "LDV0", "LDV1",
                     "LDG0", "LDG1", "LDT0", "LDT1", "LD30", "LD31", "LDX2",
                     "STK", "STX2", "STQ0", "STQ1", "STO", "STC",
                     "PE", "V", "A", "CC", "G"):
            self.h[name] = es.enter_context(nc.semaphore("sem_" + name))
            self.n[name] = 0
        self.ops = {e: [] for e in ENGS}

    def emit(self, eng, fn, wait=(), inc=()):
        for name, k in inc:
            self.n[name] += k
        self.ops[eng].append((fn, tuple(wait), tuple(inc)))

    def flush(self, nc):
        ops = self.ops
        self.ops = {e: [] for e in ENGS}
        h = self.h

        with nc.Block() as block:
            def make_body(eng_name):
                def body(eng):
                    for fn, waits, incs in ops[eng_name]:
                        for name, val in waits:
                            if val > 0:
                                eng.wait_ge(h[name], val)
                        inst = fn(eng)
                        for name, k in incs:
                            inst.then_inc(h[name], k)
                return body

            for eng_name in ENGS:
                getattr(block, eng_name)(make_body(eng_name))


def build():
    nc = bass.Bass("TRN2", target_bir_lowering=False, debug=False,
                   detect_race_conditions=False)

    xs = nc.dram_tensor("xs", [NT, 128, MC, TT], F32, kind="ExternalInput").ap()
    wqs = nc.dram_tensor("wqs", [OC, 128, MC, 128], F32, kind="ExternalInput").ap()
    wks = nc.dram_tensor("wks", [G, 128, MC, 128], F32, kind="ExternalInput").ap()
    wvs = nc.dram_tensor("wvs", [128, MC, 512], F32, kind="ExternalInput").ap()
    wos = nc.dram_tensor("wos", [OC, 128, MC, 128], F32, kind="ExternalInput").ap()
    w1s = nc.dram_tensor("w1s", [FHC, 128, MC, 128], F32, kind="ExternalInput").ap()
    w2s = nc.dram_tensor("w2s", [FHC, 128, MC, 128], F32, kind="ExternalInput").ap()
    w3s = nc.dram_tensor("w3s", [OC, 2, 128, HHALF, 128], F32, kind="ExternalInput").ap()
    meta = nc.dram_tensor("meta", [128, 8], F32, kind="ExternalInput").ap()

    out = nc.dram_tensor("out", [NT, 128, OC, TT], F32, kind="ExternalOutput").ap()

    _skind = "ExternalOutput" if DEBUG_SCRATCH else "Internal"
    kf_scr = nc.dram_tensor("kf_scr", [NT, 128, G, TT], F32, kind=_skind).ap()
    v_scr = nc.dram_tensor("v_scr", [NT, 128, 2, 512], F32, kind=_skind).ap()
    x2_scr = nc.dram_tensor("x2_scr", [NT, 128, MC, TT], F32, kind=_skind).ap()
    st_scr = nc.dram_tensor("st_scr", [NT, 128, 2, TT], F32, kind=_skind).ap()
    tw1 = nc.dram_tensor("tw1", [FHC, 128, MC, 128], BF16, kind=_skind).ap()
    tw2 = nc.dram_tensor("tw2", [FHC, 128, MC, 128], BF16, kind=_skind).ap()
    tw3 = nc.dram_tensor("tw3", [OC, 2, 128, HHALF, 128], BF16, kind=_skind).ap()
    if DEBUG_SCRATCH:
        qf_dbg = nc.dram_tensor("qf_dbg", [NT, 128, H, TT], F32, kind="ExternalOutput").ap()
        oh_dbg = nc.dram_tensor("oh_dbg", [NT, 128, H, TT], F32, kind="ExternalOutput").ap()
        ar_dbg = nc.dram_tensor("ar_dbg", [NT, 128, OC, TT], F32, kind="ExternalOutput").ap()
        tbl_dbg = nc.dram_tensor("tbl_dbg", [64, 2 * T], F32, kind="ExternalOutput").ap()
        thr_dbg = nc.dram_tensor("thr_dbg", [128, 12], F32, kind="ExternalOutput").ap()
        gacc_dbg = nc.dram_tensor("gacc_dbg", [3, 128, 4], F32, kind="ExternalOutput").ap()
    cc_in = nc.dram_tensor("cc_in", [128, SW], F32)
    cc_out = nc.dram_tensor("cc_out", [512, SW], F32)

    pb = [nc.alloc_psum_tensor(f"pb{i}", [128, 512], F32) for i in range(8)]

    with contextlib.ExitStack() as top:
        E = Emitter(nc, top)

        ones = nc.alloc_sbuf_tensor("ones", [128, 128], F32)
        ident = nc.alloc_sbuf_tensor("ident", [128, 128], F32)
        maskge = nc.alloc_sbuf_tensor("maskge", [128, 128], F32)
        meta_sb = nc.alloc_sbuf_tensor("meta_sb", [128, 8], F32)
        invf = nc.alloc_sbuf_tensor("invf", [64, 1], F32)
        iof = nc.alloc_sbuf_tensor("iof", [64, 1], F32)
        thr = nc.alloc_sbuf_tensor("thr", [128, 4], F32)
        nthr = nc.alloc_sbuf_tensor("nthr", [128, 4], F32)
        gacc = nc.alloc_sbuf_tensor("gacc", [128, 4], F32)
        gredbuf = nc.alloc_sbuf_tensor("gredbuf", [128, 128], F32)
        totsb = nc.alloc_sbuf_tensor("totsb", [1, 4], F32)

        def emit_stats(src_fn, width, mu_t, is_t, xsqb, bk_s, bk_q,
                       pe_wait=(), src_wait=()):
            nb = len(xsqb)
            a_marks = []
            pe_sum0 = E.n["PE"]
            for i in range(MC):
                w = list(src_wait)
                if i >= nb:
                    w.append(("PE", pe_sum0 + MC + (i - nb) + 1))
                E.emit("scalar",
                       (lambda i: lambda e: e.activation(
                           out=xsqb[i % nb][:, 0:width], in_=src_fn(i),
                           func=AF.Square))(i),
                       wait=w, inc=[("A", 1)])
                a_marks.append(E.n["A"])
            for i in range(MC):
                E.emit("tensor",
                       (lambda i: lambda e: e.matmul(
                           bk_s[:, 0:width], ones[:], src_fn(i),
                           start=(i == 0), stop=(i == MC - 1)))(i),
                       wait=(list(pe_wait) if i == 0 else []),
                       inc=[("PE", 1)])
            for i in range(MC):
                E.emit("tensor",
                       (lambda i, am: lambda e: e.matmul(
                           bk_q[:, 0:width], ones[:], xsqb[i % nb][:, 0:width],
                           start=(i == 0), stop=(i == MC - 1)))(i, a_marks[i]),
                       wait=[("A", a_marks[i])], inc=[("PE", 1)])
            pe_done = E.n["PE"]
            E.emit("vector", lambda e: e.tensor_scalar_mul(
                mu_t[:, 0:width], bk_s[:, 0:width], 1.0 / D),
                wait=[("PE", pe_done)])
            E.emit("vector", lambda e: e.tensor_scalar_mul(
                is_t[:, 0:width], bk_q[:, 0:width], 1.0 / D))
            E.emit("vector", lambda e: e.tensor_tensor(
                xsqb[0][:, 0:width], mu_t[:, 0:width], mu_t[:, 0:width], ALU.mult))
            E.emit("vector", lambda e: e.tensor_tensor(
                is_t[:, 0:width], is_t[:, 0:width], xsqb[0][:, 0:width],
                ALU.subtract))
            E.emit("vector", lambda e: e.tensor_scalar_add(
                is_t[:, 0:width], is_t[:, 0:width], LN_EPS))
            E.emit("vector", lambda e: e.reciprocal(
                is_t[:, 0:width], is_t[:, 0:width]), inc=[("V", 1)])
            vr = E.n["V"]
            E.emit("scalar", lambda e: e.activation(
                out=is_t[:, 0:width], in_=is_t[:, 0:width], func=AF.Sqrt),
                wait=[("V", vr)], inc=[("A", 1)])
            return E.n["A"], pe_done

        with contextlib.ExitStack() as p12:
            cosT = p12.enter_context(nc.sbuf_tensor("cosT", [64, T], F32))
            sinT = p12.enter_context(nc.sbuf_tensor("sinT", [64, T], F32))
            SS = p12.enter_context(nc.sbuf_tensor("SS", [128, NCH * SW], F32))
            gbuf = [p12.enter_context(nc.sbuf_tensor(f"gbuf{i}", [128, 2816], F32))
                    for i in range(2)]
            qbuf = [p12.enter_context(nc.sbuf_tensor(f"qbuf{i}", [128, 2816], BF16))
                    for i in range(2)]
            m2tmp = p12.enter_context(nc.sbuf_tensor("m2tmp", [128, 2816], BF16))

            with contextlib.ExitStack() as p0tmp:
                thb = p0tmp.enter_context(nc.sbuf_tensor("thb", [64, T], F32))
                r1 = p0tmp.enter_context(nc.sbuf_tensor("r1", [64, T], F32))
                r2 = p0tmp.enter_context(nc.sbuf_tensor("r2", [64, T], F32))
                ri = p0tmp.enter_context(nc.sbuf_tensor("ri", [64, T], mybir.dt.int32))

                E.emit("sync", lambda e: e.dma_start(out=meta_sb[:], in_=meta),
                       inc=[("LDX", 16)])
                ld_meta = E.n["LDX"]

                E.emit("gpsimd", lambda e: e.memset(ones[:], 1.0), inc=[("G", 1)])
                st1 = E.n["G"]
                E.emit("gpsimd", lambda e: e.affine_select(
                    ident[:], ones[:], [[1, 128]], ALU.is_equal, 0.0,
                    base=0, channel_multiplier=-1),
                    wait=[("G", st1)], inc=[("G", 1)])
                E.emit("gpsimd", lambda e: e.affine_select(
                    maskge[:], ones[:], [[1, 128]], ALU.is_ge, 0.0,
                    base=0, channel_multiplier=-1),
                    wait=[("G", st1)], inc=[("G", 1)])
                E.emit("gpsimd", lambda e: e.iota(
                    iof[:], [[0, 1]], channel_multiplier=1,
                    allow_small_or_imprecise_dtypes=True), inc=[("G", 1)])
                st_iof = E.n["G"]
                E.emit("gpsimd", lambda e: e.iota(
                    thb[:], [[1, T]], channel_multiplier=0,
                    allow_small_or_imprecise_dtypes=True), inc=[("G", 1)])
                st_thb = E.n["G"]

                E.emit("scalar", lambda e: e.activation(
                    out=invf[:], in_=iof[:], func=AF.Exp,
                    scale=float(-np.log(ROPE_BASE) / 64.0)),
                    wait=[("G", st_iof)], inc=[("A", 1)])
                a_invf = E.n["A"]

                TWO_PI, PI = float(2 * np.pi), float(np.pi)
                E.emit("vector", lambda e: e.tensor_scalar_add(
                    thb[:], thb[:], meta_sb[0:64, 0:1]),
                    wait=[("LDX", ld_meta), ("G", st_thb), ("A", a_invf)])
                E.emit("vector", lambda e: e.tensor_tensor(
                    thb[:], thb[:], invf[:].broadcast_to((64, T)), ALU.mult))
                E.emit("vector", lambda e: e.tensor_scalar_mul(
                    r1[:], thb[:], float(1.0 / (2 * np.pi))))
                E.emit("vector", lambda e: e.tensor_copy(ri[:], r1[:]))
                E.emit("vector", lambda e: e.tensor_copy(r1[:], ri[:]))
                E.emit("vector", lambda e: e.scalar_tensor_tensor(
                    r1[:], r1[:], -TWO_PI, thb[:], ALU.mult, ALU.add))
                E.emit("vector", lambda e: e.tensor_single_scalar(
                    r2[:], r1[:], PI, ALU.is_gt))
                E.emit("vector", lambda e: e.scalar_tensor_tensor(
                    r1[:], r2[:], -TWO_PI, r1[:], ALU.mult, ALU.add))
                E.emit("vector", lambda e: e.tensor_single_scalar(
                    r2[:], r1[:], -PI, ALU.is_lt))
                E.emit("vector", lambda e: e.scalar_tensor_tensor(
                    r1[:], r2[:], TWO_PI, r1[:], ALU.mult, ALU.add),
                    inc=[("V", 1)])
                v_s = E.n["V"]
                E.emit("scalar", lambda e: e.activation(
                    out=sinT[:], in_=r1[:], func=AF.Sin),
                    wait=[("V", v_s)], inc=[("A", 1)])
                a_sin = E.n["A"]
                E.emit("scalar", lambda e: e.activation(
                    out=r2[:], in_=r1[:], func=AF.Abs),
                    wait=[("V", v_s)], inc=[("A", 1)])
                a_abs = E.n["A"]
                E.emit("vector", lambda e: e.tensor_scalar(
                    r2[:], r2[:], -1.0, float(np.pi / 2), ALU.mult, ALU.add),
                    wait=[("A", a_abs)], inc=[("V", 1)])
                v_c = E.n["V"]
                E.emit("scalar", lambda e: e.activation(
                    out=cosT[:], in_=r2[:], func=AF.Sin),
                    wait=[("V", v_c)], inc=[("A", 1)])
                a_cos = E.n["A"]
                if DEBUG_SCRATCH:
                    E.emit("gpsimd", lambda e: e.dma_start(
                        out=tbl_dbg[:, 0:T], in_=cosT[:]),
                        wait=[("A", a_cos)], inc=[("STO", 16)])
                    E.emit("gpsimd", lambda e: e.dma_start(
                        out=tbl_dbg[:, T:2 * T], in_=sinT[:]),
                        wait=[("A", a_cos)], inc=[("STO", 16)])

                E.flush(nc)

            gamma_chunks = ([(w1s[hc], 2048, 0) for hc in range(FHC)]
                            + [(w2s[hc], 2048, 1) for hc in range(FHC)]
                            + [(w3s[oc, hf], 2816, 2)
                               for oc in range(OC) for hf in range(2)])
            quant_chunks = ([(w1s[hc], tw1[hc], 2048, 0) for hc in range(FHC)]
                            + [(w2s[hc], tw2[hc], 2048, 1) for hc in range(FHC)]
                            + [(w3s[oc, hf], tw3[oc, hf], 2816, 2)
                               for oc in range(OC) for hf in range(2)])

            gst = {"k": 0, "v": [0, 0]}

            def emit_gamma_chunk():
                k = gst["k"]
                if k >= len(gamma_chunks):
                    return
                src, w, wi = gamma_chunks[k]
                buf = k % 2
                E.emit("sync",
                       (lambda src, buf, w: lambda e: e.dma_start(
                           out=gbuf[buf][:, 0:w], in_=src))(src, buf, w),
                       wait=[("V", gst["v"][buf])], inc=[("LDG%d" % buf, 16)])
                ldk = E.n["LDG%d" % buf]
                E.emit("vector",
                       (lambda buf, w, k: lambda e: e.reduce_sum(
                           gredbuf[:, k:k + 1], gbuf[buf][:, 0:w],
                           mybir.AxisListType.X, apply_absolute_value=True))(buf, w, k),
                       wait=[("LDG%d" % buf, ldk)], inc=[("V", 1)])
                gst["v"][buf] = E.n["V"]
                gst["k"] = k + 1
                if DEBUG_SCRATCH and (k + 1) in (FHC, 2 * FHC, 3 * FHC - 12 + 12, len(gamma_chunks)):
                    idx = {FHC: 0, 2 * FHC: 1, len(gamma_chunks): 2}.get(k + 1)
                    if idx is not None:
                        vv = E.n["V"]
                        E.emit("gpsimd",
                               (lambda idx: lambda e: e.dma_start(
                                   out=gacc_dbg[idx], in_=gacc[:]))(idx),
                               wait=[("V", vv)], inc=[("STO", 16)])

            GCOL = [(0, FHC), (FHC, 2 * FHC), (2 * FHC, 2 * FHC + 2 * OC)]

            def emit_thresholds(pe_bank_wait):
                vw = E.n["V"]
                for wi in range(3):
                    lo, hi = GCOL[wi]
                    E.emit("vector",
                           (lambda wi, lo, hi: lambda e: e.reduce_sum(
                               gacc[:, wi:wi + 1], gredbuf[:, lo:hi],
                               mybir.AxisListType.X))(wi, lo, hi),
                           wait=([("V", vw)] if wi == 0 else []),
                           inc=[("V", 1)])
                v_gacc = E.n["V"]
                for wi in range(3):
                    E.emit("tensor",
                           (lambda wi: lambda e: e.matmul(
                               pb[0][0:1, wi:wi + 1], gacc[:, wi:wi + 1],
                               ones[:, 0:1],
                               start=(wi == 0), stop=(wi == 2)))(wi),
                           wait=([("V", v_gacc)] + list(pe_bank_wait))
                           if wi == 0 else [], inc=[("PE", 1)])
                p1_ = E.n["PE"]
                E.emit("vector",
                       lambda e: e.tensor_copy(totsb[0:1, 0:3], pb[0][0:1, 0:3]),
                       wait=[("PE", p1_)], inc=[("V", 1)])
                v1 = E.n["V"]
                E.emit("tensor",
                       lambda e: e.matmul(
                           pb[0][:, 0:3], ones[0:1, :], totsb[0:1, 0:3],
                           start=True, stop=True),
                       wait=[("V", v1)], inc=[("PE", 1)])
                p2_ = E.n["PE"]
                E.emit("vector",
                       lambda e: e.tensor_scalar(
                           thr[:, 0:3], pb[0][:, 0:3],
                           0.5 / NW, 0.5e-8, ALU.mult, ALU.add),
                       wait=[("PE", p2_)])
                E.emit("vector",
                       lambda e: e.tensor_scalar(
                           nthr[:, 0:3], pb[0][:, 0:3],
                           -0.5 / NW, -0.5e-8, ALU.mult, ALU.add),
                       inc=[("V", 1)])
                return E.n["V"]

            qst = {"k": 0, "v": [0, 0], "st": [0, 0], "thr_v": None}

            def emit_quant_chunk():
                k = qst["k"]
                if k >= len(quant_chunks):
                    return
                src, dst, w, wi = quant_chunks[k]
                buf = k % 2
                E.emit("sync",
                       (lambda src, buf, w: lambda e: e.dma_start(
                           out=gbuf[buf][:, 0:w], in_=src))(src, buf, w),
                       wait=[("V", qst["v"][buf])], inc=[("LDG%d" % buf, 16)])
                ldk = E.n["LDG%d" % buf]
                E.emit("vector",
                       (lambda buf, w, wi: lambda e: e.tensor_single_scalar(
                           qbuf[buf][:, 0:w], gbuf[buf][:, 0:w],
                           thr[:, wi:wi + 1], ALU.is_gt))(buf, w, wi),
                       wait=[("LDG%d" % buf, ldk), ("V", qst["thr_v"]),
                             ("STQ%d" % buf, qst["st"][buf])])
                E.emit("vector",
                       (lambda buf, w, wi: lambda e: e.tensor_single_scalar(
                           m2tmp[:, 0:w], gbuf[buf][:, 0:w],
                           nthr[:, wi:wi + 1], ALU.is_lt))(buf, w, wi))
                E.emit("vector",
                       (lambda buf, w: lambda e: e.tensor_tensor(
                           qbuf[buf][:, 0:w], qbuf[buf][:, 0:w],
                           m2tmp[:, 0:w], ALU.subtract))(buf, w),
                       inc=[("V", 1)])
                qst["v"][buf] = E.n["V"]
                E.emit("gpsimd",
                       (lambda dst, buf, w: lambda e: e.dma_start(
                           out=dst, in_=qbuf[buf][:, 0:w]))(dst, buf, w),
                       wait=[("V", qst["v"][buf])], inc=[("STQ%d" % buf, 16)])
                qst["st"][buf] = E.n["STQ%d" % buf]
                qst["k"] = k + 1

            with contextlib.ExitStack() as p1es:
                xt = p1es.enter_context(nc.sbuf_tensor("xt", [128, MC * TT], F32))
                ht = p1es.enter_context(nc.sbuf_tensor("ht", [128, MC * TT], F32))
                qft = p1es.enter_context(nc.sbuf_tensor("qft", [128, H * TT], F32))
                outh = p1es.enter_context(nc.sbuf_tensor("outh", [128, H * TT], F32))
                kft = p1es.enter_context(nc.sbuf_tensor("kft", [128, G * TT], F32))
                vt = p1es.enter_context(nc.sbuf_tensor("vt", [128, 2 * 512], F32))
                wkb = [p1es.enter_context(nc.sbuf_tensor(f"wkb{i}", [128, 2048], F32))
                       for i in range(2)]
                wvb = [p1es.enter_context(nc.sbuf_tensor(f"wvb{i}", [128, 512], F32))
                       for i in range(2)]
                xsqb = [p1es.enter_context(nc.sbuf_tensor(f"xsq{i}", [128, TT], F32))
                        for i in range(8)]
                murep = p1es.enter_context(nc.sbuf_tensor("murep", [128, TT], F32))
                isrep = p1es.enter_context(nc.sbuf_tensor("isrep", [128, TT], F32))
                rt1 = p1es.enter_context(nc.sbuf_tensor("rt1", [64, TT], F32))
                rt2 = p1es.enter_context(nc.sbuf_tensor("rt2", [64, TT], F32))
                t3s = p1es.enter_context(nc.sbuf_tensor("t3s", [128, TT], F32))
                texp = p1es.enter_context(nc.sbuf_tensor("texp", [128, TT], F32))
                kfT = p1es.enter_context(nc.sbuf_tensor("kfT", [128, 128], F32))
                msb = p1es.enter_context(nc.sbuf_tensor("msb", [128, 128], F32))
                dsb = p1es.enter_context(nc.sbuf_tensor("dsb", [128, 128], F32))
                ccg = p1es.enter_context(nc.sbuf_tensor("ccg", [128, 4 * SW], F32))
                spfx = p1es.enter_context(nc.sbuf_tensor("spfx", [128, SW], F32))

                def emit_rope_elu(ps_bank, dst, dcol, tt, pe_w):
                    cs = cosT[:, tt * TT:(tt + 1) * TT]
                    sn = sinT[:, tt * TT:(tt + 1) * TT]
                    E.emit("vector",
                           (lambda ps_bank, cs: lambda e: e.tensor_mul(
                               rt1[:], ps_bank[0:64, 0:TT], cs))(ps_bank, cs),
                           wait=pe_w)
                    E.emit("vector",
                           (lambda ps_bank, sn: lambda e: e.tensor_mul(
                               rt2[:], ps_bank[64:128, 0:TT], sn))(ps_bank, sn))
                    E.emit("vector",
                           (lambda dst, dcol: lambda e: e.tensor_sub(
                               dst[0:64, dcol:dcol + TT], rt1[:], rt2[:]))(dst, dcol))
                    E.emit("vector",
                           (lambda ps_bank, cs: lambda e: e.tensor_mul(
                               rt1[:], ps_bank[64:128, 0:TT], cs))(ps_bank, cs))
                    E.emit("vector",
                           (lambda ps_bank, sn: lambda e: e.tensor_mul(
                               rt2[:], ps_bank[0:64, 0:TT], sn))(ps_bank, sn))
                    E.emit("vector",
                           (lambda dst, dcol: lambda e: e.tensor_add(
                               dst[64:128, dcol:dcol + TT], rt1[:], rt2[:]))(dst, dcol))
                    E.emit("vector",
                           (lambda dst, dcol: lambda e: e.tensor_scalar_min(
                               t3s[:], dst[:, dcol:dcol + TT], 0.0))(dst, dcol),
                           inc=[("V", 1)])
                    vr = E.n["V"]
                    E.emit("scalar",
                           lambda e: e.activation(out=texp[:], in_=t3s[:], func=AF.Exp),
                           wait=[("V", vr)], inc=[("A", 1)])
                    ae = E.n["A"]
                    E.emit("vector",
                           (lambda dst, dcol: lambda e: e.scalar_tensor_tensor(
                               dst[:, dcol:dcol + TT], dst[:, dcol:dcol + TT], 0.0,
                               texp[:], ALU.max, ALU.add))(dst, dcol),
                           wait=[("A", ae)], inc=[("V", 1)])
                    return E.n["V"]

                prev_h = 0
                prev_ss = 0
                st_kf_prev = 0
                st_v_prev = 0
                st_stats_prev = 0
                wk_pe = [0, 0]
                wv_pe = [0, 0]

                for tt in range(NT):
                    E.emit("sync",
                           (lambda tt: lambda e: e.dma_start(out=xt[:], in_=xs[tt]))(tt),
                           wait=[("V", prev_h), ("A", a_cos)], inc=[("LDX", 16)])
                    ld_x = E.n["LDX"]

                    a_st, pe_st = emit_stats(
                        (lambda i: xt[:, i * TT:(i + 1) * TT]), TT,
                        murep, isrep, xsqb, pb[0], pb[1],
                        pe_wait=[("LDX", ld_x), ("V", prev_ss),
                                 ("STK", st_stats_prev)],
                        src_wait=[("LDX", ld_x)])
                    E.emit("gpsimd",
                           (lambda tt: lambda e: e.dma_start(
                               out=st_scr[tt][:, 0, :], in_=murep[:]))(tt),
                           wait=[("A", a_st)], inc=[("STK", 16)])
                    E.emit("gpsimd",
                           (lambda tt: lambda e: e.dma_start(
                               out=st_scr[tt][:, 1, :], in_=isrep[:]))(tt),
                           inc=[("STK", 16)])
                    st_stats_prev = E.n["STK"]

                    for i in range(MC):
                        E.emit("vector",
                               (lambda i: lambda e: e.tensor_sub(
                                   ht[:, i * TT:(i + 1) * TT],
                                   xt[:, i * TT:(i + 1) * TT], murep[:]))(i),
                               wait=([("A", a_st)] if i == 0 else []))
                        E.emit("vector",
                               (lambda i: lambda e: e.tensor_mul(
                                   ht[:, i * TT:(i + 1) * TT],
                                   ht[:, i * TT:(i + 1) * TT], isrep[:]))(i),
                               inc=([("V", 1)] if i == MC - 1 else []))
                    prev_h = E.n["V"]
                    v_h = E.n["V"]

                    rope_done = 0
                    for g in range(G):
                        buf = g % 2
                        E.emit("sync",
                               (lambda g, buf: lambda e: e.dma_start(
                                   out=wkb[buf][:], in_=wks[g]))(g, buf),
                               wait=[("PE", wk_pe[buf])], inc=[("LDW%d" % buf, 16)])
                        ld_wk = E.n["LDW%d" % buf]
                        for i in range(MC):
                            w = [("LDW%d" % buf, ld_wk)]
                            if i == 0:
                                w += [("V", v_h), ("V", rope_done),
                                      ("STK", st_kf_prev)]
                            E.emit("tensor",
                                   (lambda g, buf, i: lambda e: e.matmul(
                                       pb[2 + g % 2][:, 0:TT],
                                       wkb[buf][:, i * 128:(i + 1) * 128],
                                       ht[:, i * TT:(i + 1) * TT],
                                       start=(i == 0), stop=(i == MC - 1)))(g, buf, i),
                                   wait=w, inc=([("PE", 1)] if i == MC - 1 else []))
                        wk_pe[buf] = E.n["PE"]
                        rope_done = emit_rope_elu(
                            pb[2 + g % 2], kft, g * TT, tt,
                            [("PE", wk_pe[buf])])
                    v_kf = E.n["V"]

                    a_ev = []
                    for sc in range(2):
                        for i in range(MC):
                            buf = i % 2
                            E.emit("sync",
                                   (lambda i, buf: lambda e: e.dma_start(
                                       out=wvb[buf][:], in_=wvs[:, i, :]))(i, buf),
                                   wait=[("PE", wv_pe[buf])], inc=[("LDV%d" % buf, 16)])
                            ld_wv = E.n["LDV%d" % buf]
                            w = [("LDV%d" % buf, ld_wv)]
                            if i == 0:
                                w.append(("A", a_ev[-2] if len(a_ev) >= 2 else 0))
                                w.append(("STK", st_v_prev))
                            E.emit("tensor",
                                   (lambda sc, i, buf: lambda e: e.matmul(
                                       pb[4 + sc][:, 0:512],
                                       ht[:, i * TT + sc * 128: i * TT + (sc + 1) * 128],
                                       wvb[buf][:],
                                       start=(i == 0), stop=(i == MC - 1)))(sc, i, buf),
                                   wait=w, inc=[("PE", 1)])
                            wv_pe[buf] = E.n["PE"]
                        pe_v = E.n["PE"]
                        E.emit("scalar",
                               (lambda sc: lambda e: e.copy(
                                   vt[:, sc * 512:(sc + 1) * 512],
                                   pb[4 + sc][:, 0:512]))(sc),
                               wait=[("PE", pe_v)], inc=[("A", 1)])
                        a_ev.append(E.n["A"])
                    a_v = E.n["A"]

                    E.emit("gpsimd",
                           (lambda tt: lambda e: e.dma_start(
                               out=kf_scr[tt], in_=kft[:]))(tt),
                           wait=[("V", v_kf)], inc=[("STK", 16)])
                    E.emit("gpsimd",
                           (lambda tt: lambda e: e.dma_start(
                               out=v_scr[tt], in_=vt[:]))(tt),
                           wait=[("A", a_v)], inc=[("STK", 16)])
                    st_kf_prev = st_v_prev = E.n["STK"]

                    for c in range(2):
                        gc = tt * 2 + c
                        v_kfT = 0
                        for g in range(G):
                            E.emit("tensor",
                                   (lambda g, c: lambda e: e.transpose(
                                       pb[6][:, 0:128],
                                       kft[:, g * TT + c * 128: g * TT + (c + 1) * 128],
                                       ident[:]))(g, c),
                                   wait=[("V", v_kf), ("V", v_kfT)],
                                   inc=[("PE", 1)])
                            pe_tr = E.n["PE"]
                            E.emit("vector",
                                   lambda e: e.tensor_copy(kfT[:], pb[6][:, 0:128]),
                                   wait=[("PE", pe_tr)], inc=[("V", 1)])
                            v_kfT = E.n["V"]
                            E.emit("tensor",
                                   (lambda g, c: lambda e: e.matmul(
                                       pb[7][:, g * 128:(g + 1) * 128], kfT[:],
                                       vt[:, c * 512 + g * 128: c * 512 + (g + 1) * 128],
                                       start=(g == 0), stop=(g == G - 1)))(g, c),
                                   wait=[("V", v_kfT)]
                                   + ([("V", prev_ss)] if g == 0 else []))
                            E.emit("tensor",
                                   (lambda g: lambda e: e.matmul(
                                       pb[0][:, g:g + 1], kfT[:], ones[:, 0:1],
                                       start=(g == 0), stop=(g == G - 1)))(g),
                                   inc=([("PE", 1)] if g == G - 1 else []))
                        pe_ds = E.n["PE"]
                        base, pbase = gc * SW, (gc - 1) * SW
                        for g in range(G):
                            w = [("PE", pe_ds)] if g == 0 else []
                            if gc == 0:
                                E.emit("vector",
                                       (lambda base, g: lambda e: e.tensor_copy(
                                           SS[:, base + g * 129: base + g * 129 + 128],
                                           pb[7][:, g * 128:(g + 1) * 128]))(base, g),
                                       wait=w)
                                E.emit("vector",
                                       (lambda base, g: lambda e: e.tensor_copy(
                                           SS[:, base + g * 129 + 128: base + g * 129 + 129],
                                           pb[0][:, g:g + 1]))(base, g),
                                       inc=([("V", 1)] if g == G - 1 else []))
                            else:
                                E.emit("vector",
                                       (lambda base, pbase, g: lambda e: e.tensor_add(
                                           SS[:, base + g * 129: base + g * 129 + 128],
                                           SS[:, pbase + g * 129: pbase + g * 129 + 128],
                                           pb[7][:, g * 128:(g + 1) * 128]))(base, pbase, g),
                                       wait=w)
                                E.emit("vector",
                                       (lambda base, pbase, g: lambda e: e.tensor_add(
                                           SS[:, base + g * 129 + 128: base + g * 129 + 129],
                                           SS[:, pbase + g * 129 + 128: pbase + g * 129 + 129],
                                           pb[0][:, g:g + 1]))(base, pbase, g),
                                       inc=([("V", 1)] if g == G - 1 else []))
                        prev_ss = E.n["V"]

                    for _ in range(15):
                        emit_gamma_chunk()

                while gst["k"] < len(gamma_chunks):
                    emit_gamma_chunk()
                qst["thr_v"] = emit_thresholds(pe_bank_wait=[("V", prev_ss)])
                if DEBUG_SCRATCH:
                    def _dump_thr(e):
                        return e.dma_start(out=thr_dbg[:, 0:4], in_=thr[:])
                    E.emit("gpsimd", _dump_thr, wait=[("V", qst["thr_v"])],
                           inc=[("STO", 16)])
                    E.emit("gpsimd", lambda e: e.dma_start(
                        out=thr_dbg[:, 4:8], in_=gacc[:]), inc=[("STO", 16)])
                    E.emit("gpsimd", lambda e: e.dma_start(
                        out=thr_dbg[:, 8:12], in_=nthr[:]), inc=[("STO", 16)])

                quiesce = [(nm, E.n[nm]) for nm in
                           ("PE", "V", "A", "G", "LDX", "LDK", "LDS", "LDW0",
                            "LDW1", "LDV0", "LDV1", "LDG0", "LDG1", "STK",
                            "STX2", "STO", "CC")]
                E.emit("gpsimd",
                       lambda e: e.dma_start(
                           out=cc_in.ap(), in_=SS[:, (NCH - 1) * SW: NCH * SW]),
                       wait=[("V", prev_ss)] + quiesce, inc=[("STC", 16)])
                st_cc = E.n["STC"]
                if not _SKIP_CC:
                    E.emit("gpsimd",
                           lambda e: e.collective_compute(
                               "AllGather", ALU.bypass,
                               replica_groups=[[0, 1, 2, 3], [4, 5, 6, 7]],
                               ins=[cc_in.ap().opt()],
                               outs=[cc_out.ap().opt()]),
                           wait=[("STC", st_cc)], inc=[("CC", 1)])
                cc1 = E.n["CC"]
                for eng in ("sync", "tensor", "vector", "scalar"):
                    E.emit(eng, lambda e: None, wait=[("CC", cc1)])
                for j in range(4):
                    E.emit("gpsimd",
                           (lambda j: lambda e: e.dma_start(
                               out=ccg[:, j * SW:(j + 1) * SW],
                               in_=cc_out.ap()[j * 128:(j + 1) * 128, :]))(j),
                           wait=[("CC", cc1)], inc=[("STC", 16)])
                st_ccg = E.n["STC"]
                E.emit("vector",
                       lambda e: e.tensor_single_scalar(
                           spfx[:], ccg[:, 0:SW], meta_sb[:, 1:2], ALU.mult),
                       wait=[("STC", st_ccg)])
                for j in range(1, 4):
                    E.emit("vector",
                           (lambda j: lambda e: e.scalar_tensor_tensor(
                               spfx[:], ccg[:, j * SW:(j + 1) * SW],
                               meta_sb[:, 1 + j:2 + j], spfx[:],
                               ALU.mult, ALU.add))(j))
                for gc in range(NCH - 1, 0, -1):
                    E.emit("vector",
                           (lambda gc: lambda e: e.tensor_add(
                               SS[:, gc * SW:(gc + 1) * SW],
                               SS[:, (gc - 1) * SW: gc * SW], spfx[:]))(gc))
                E.emit("vector",
                       lambda e: e.tensor_copy(SS[:, 0:SW], spfx[:]),
                       inc=[("V", 1)])
                v_ssready = E.n["V"]

                st_p1 = E.n["STK"]
                pe_p1 = E.n["PE"]
                a_p1 = E.n["A"]

                wq_pe = [0, 0]
                prev_x2v = E.n["V"]
                prev_scan = E.n["V"]
                prev_a_wo = 0
                st_x2_prev = 0

                for tt in range(0 if _SKIP_P2 else NT):
                    first_w = ([("STK", st_p1), ("PE", pe_p1), ("A", a_p1)]
                               if tt == 0 else [])
                    E.emit("sync",
                           (lambda tt: lambda e: e.dma_start(out=xt[:], in_=xs[tt]))(tt),
                           wait=[("V", prev_x2v), ("STX2", st_x2_prev)] + first_w,
                           inc=[("LDX", 16)])
                    ld_x = E.n["LDX"]
                    E.emit("sync",
                           (lambda tt: lambda e: e.dma_start(
                               out=kft[:], in_=kf_scr[tt]))(tt),
                           wait=[("V", prev_scan)] + first_w, inc=[("LDK", 16)])
                    E.emit("sync",
                           (lambda tt: lambda e: e.dma_start(
                               out=vt[:], in_=v_scr[tt]))(tt),
                           inc=[("LDK", 16)])
                    ld_kv = E.n["LDK"]
                    E.emit("sync",
                           (lambda tt: lambda e: e.dma_start(
                               out=murep[:], in_=st_scr[tt][:, 0, :]))(tt),
                           wait=[("V", prev_x2v)] + first_w, inc=[("LDS", 16)])
                    E.emit("sync",
                           (lambda tt: lambda e: e.dma_start(
                               out=isrep[:], in_=st_scr[tt][:, 1, :]))(tt),
                           inc=[("LDS", 16)])
                    ld_st = E.n["LDS"]

                    for i in range(MC):
                        E.emit("vector",
                               (lambda i: lambda e: e.tensor_sub(
                                   ht[:, i * TT:(i + 1) * TT],
                                   xt[:, i * TT:(i + 1) * TT], murep[:]))(i),
                               wait=([("LDX", ld_x), ("LDS", ld_st)]
                                     if i == 0 else []))
                        E.emit("vector",
                               (lambda i: lambda e: e.tensor_mul(
                                   ht[:, i * TT:(i + 1) * TT],
                                   ht[:, i * TT:(i + 1) * TT], isrep[:]))(i),
                               inc=([("V", 1)] if i == MC - 1 else []))
                    v_h = E.n["V"]

                    rope_done = 0
                    for oc in range(H):
                        buf = oc % 2
                        E.emit("sync",
                               (lambda oc, buf: lambda e: e.dma_start(
                                   out=wkb[buf][:], in_=wqs[oc]))(oc, buf),
                               wait=[("PE", wq_pe[buf])], inc=[("LDW%d" % buf, 16)])
                        ld_wq = E.n["LDW%d" % buf]
                        for i in range(MC):
                            w = [("LDW%d" % buf, ld_wq)]
                            if i == 0:
                                w += [("V", v_h), ("V", rope_done),
                                      ("A", prev_a_wo)]
                            E.emit("tensor",
                                   (lambda oc, buf, i: lambda e: e.matmul(
                                       pb[2 + oc % 2][:, 0:TT],
                                       wkb[buf][:, i * 128:(i + 1) * 128],
                                       ht[:, i * TT:(i + 1) * TT],
                                       start=(i == 0), stop=(i == MC - 1)))(oc, buf, i),
                                   wait=w, inc=([("PE", 1)] if i == MC - 1 else []))
                        wq_pe[buf] = E.n["PE"]
                        rope_done = emit_rope_elu(
                            pb[2 + oc % 2], qft, oc * TT, tt,
                            [("PE", wq_pe[buf])])
                    v_qf = E.n["V"]
                    if DEBUG_SCRATCH:
                        E.emit("gpsimd",
                               (lambda tt: lambda e: e.dma_start(
                                   out=qf_dbg[tt], in_=qft[:]))(tt),
                               wait=[("V", v_qf)], inc=[("STO", 16)])
                        dbg_qf_st = E.n["STO"]

                    v_msb = 0
                    v_out = 0
                    for c in range(2):
                        gc = tt * 2 + c
                        sbase = gc * SW
                        for hh in range(H):
                            g = hh // 4
                            qcol = hh * TT + c * 128
                            E.emit("tensor",
                                   (lambda g, c, qcol: lambda e: e.matmul(
                                       pb[4][:, 0:128],
                                       kft[:, g * TT + c * 128: g * TT + (c + 1) * 128],
                                       qft[:, qcol:qcol + 128],
                                       start=True, stop=True))(g, c, qcol),
                                   wait=[("V", v_qf), ("V", v_msb),
                                         ("LDK", ld_kv), ("V", v_ssready)],
                                   inc=[("PE", 1)])
                            pe_a = E.n["PE"]
                            E.emit("vector",
                                   lambda e: e.tensor_mul(
                                       msb[:], pb[4][:, 0:128], maskge[:]),
                                   wait=[("PE", pe_a)], inc=[("V", 1)])
                            v_msb = E.n["V"]
                            E.emit("tensor",
                                   (lambda sbase, g, qcol: lambda e: e.matmul(
                                       pb[5][:, 0:128],
                                       SS[:, sbase + g * 129: sbase + g * 129 + 128],
                                       qft[:, qcol:qcol + 128],
                                       start=True, stop=False))(sbase, g, qcol),
                                   wait=[("V", v_out), ("V", v_msb)])
                            E.emit("tensor",
                                   (lambda c, g: lambda e: e.matmul(
                                       pb[5][:, 0:128],
                                       vt[:, c * 512 + g * 128: c * 512 + (g + 1) * 128],
                                       msb[:], start=False, stop=True))(c, g))
                            E.emit("tensor",
                                   (lambda sbase, g, qcol: lambda e: e.matmul(
                                       pb[6][:, 0:128],
                                       SS[:, sbase + g * 129 + 128: sbase + g * 129 + 129]
                                       .broadcast_to((128, 128)),
                                       qft[:, qcol:qcol + 128],
                                       start=True, stop=False))(sbase, g, qcol))
                            E.emit("tensor",
                                   lambda e: e.matmul(
                                       pb[6][:, 0:128], ones[:], msb[:],
                                       start=False, stop=True),
                                   inc=[("PE", 1)])
                            pe_nd = E.n["PE"]
                            E.emit("vector",
                                   lambda e: e.tensor_scalar_add(
                                       dsb[:], pb[6][:, 0:128], ATT_EPS),
                                   wait=[("PE", pe_nd)])
                            E.emit("vector", lambda e: e.reciprocal(dsb[:], dsb[:]))
                            E.emit("vector",
                                   (lambda hh, c: lambda e: e.tensor_mul(
                                       outh[:, hh * TT + c * 128: hh * TT + (c + 1) * 128],
                                       pb[5][:, 0:128], dsb[:]))(hh, c),
                                   inc=[("V", 1)])
                            v_out = E.n["V"]
                    v_scan = E.n["V"]
                    prev_scan = v_scan
                    if DEBUG_SCRATCH:
                        E.emit("gpsimd",
                               (lambda tt: lambda e: e.dma_start(
                                   out=oh_dbg[tt], in_=outh[:]))(tt),
                               wait=[("V", v_scan)], inc=[("STO", 16)])

                    araw = qft
                    a_wo_marks = []
                    for oc in range(OC):
                        buf = oc % 2
                        E.emit("sync",
                               (lambda oc, buf: lambda e: e.dma_start(
                                   out=wkb[buf][:], in_=wos[oc]))(oc, buf),
                               wait=[("PE", wq_pe[buf])], inc=[("LDW%d" % buf, 16)])
                        ld_wo = E.n["LDW%d" % buf]
                        for i in range(MC):
                            w = [("LDW%d" % buf, ld_wo)]
                            if i == 0:
                                w.append(("V", v_scan))
                                if DEBUG_SCRATCH:
                                    w.append(("STO", dbg_qf_st))
                                if len(a_wo_marks) >= 2:
                                    w.append(("A", a_wo_marks[-2]))
                            E.emit("tensor",
                                   (lambda oc, buf, i: lambda e: e.matmul(
                                       pb[2 + oc % 2][:, 0:TT],
                                       wkb[buf][:, i * 128:(i + 1) * 128],
                                       outh[:, i * TT:(i + 1) * TT],
                                       start=(i == 0), stop=(i == MC - 1)))(oc, buf, i),
                                   wait=w, inc=([("PE", 1)] if i == MC - 1 else []))
                        wq_pe[buf] = E.n["PE"]
                        pe_wo = E.n["PE"]
                        E.emit("scalar",
                               (lambda oc: lambda e: e.copy(
                                   araw[:, oc * TT:(oc + 1) * TT],
                                   pb[2 + oc % 2][:, 0:TT]))(oc),
                               wait=[("PE", pe_wo)], inc=[("A", 1)])
                        a_wo_marks.append(E.n["A"])
                    prev_a_wo = E.n["A"]

                    if DEBUG_SCRATCH:
                        E.emit("gpsimd",
                               (lambda tt: lambda e: e.dma_start(
                                   out=ar_dbg[tt], in_=araw[:]))(tt),
                               wait=[("A", prev_a_wo)], inc=[("STO", 16)])

                    a_st2, _ = emit_stats(
                        (lambda i: araw[:, i * TT:(i + 1) * TT]), TT,
                        murep, isrep, xsqb, pb[0], pb[1],
                        pe_wait=[("A", prev_a_wo)],
                        src_wait=[("A", prev_a_wo)])
                    for i in range(MC):
                        E.emit("vector",
                               (lambda i: lambda e: e.tensor_sub(
                                   t3s[:], araw[:, i * TT:(i + 1) * TT], murep[:]))(i),
                               wait=([("A", a_st2)] if i == 0 else []))
                        E.emit("vector",
                               lambda e: e.tensor_mul(t3s[:], t3s[:], isrep[:]))
                        E.emit("vector",
                               (lambda i: lambda e: e.tensor_add(
                                   xt[:, i * TT:(i + 1) * TT],
                                   xt[:, i * TT:(i + 1) * TT], t3s[:]))(i),
                               inc=([("V", 1)] if i == MC - 1 else []))
                    prev_x2v = E.n["V"]
                    E.emit("gpsimd",
                           (lambda tt: lambda e: e.dma_start(
                               out=x2_scr[tt], in_=xt[:]))(tt),
                           wait=[("V", prev_x2v), ("STX2", st_x2_prev)],
                           inc=[("STX2", 16)])
                    st_x2_prev = E.n["STX2"]

                    for _ in range(15):
                        emit_quant_chunk()

                while qst["k"] < len(quant_chunks):
                    emit_quant_chunk()

                stq0_all = E.n["STQ0"]
                stq1_all = E.n["STQ1"]
                stx2_all = E.n["STX2"]
                pe_p2 = E.n["PE"]
                a_p2 = E.n["A"]
                v_p2 = E.n["V"]

                E.flush(nc)

        with contextlib.ExitStack() as p3es:
            x2t = p3es.enter_context(nc.sbuf_tensor("x2t", [128, MC * 512], F32))
            h2hi = p3es.enter_context(nc.sbuf_tensor("h2hi", [128, MC * 512], BF16))
            h2lo = p3es.enter_context(nc.sbuf_tensor("h2lo", [128, MC * 512], BF16))
            hidhi = p3es.enter_context(nc.sbuf_tensor("hidhi", [128, HHALF * 512], BF16))
            hidlo = p3es.enter_context(nc.sbuf_tensor("hidlo", [128, HHALF * 512], BF16))
            fraw = p3es.enter_context(nc.sbuf_tensor("fraw", [128, OC * 512], F32))
            t1b = [p3es.enter_context(nc.sbuf_tensor(f"t1b{i}", [128, 2048], BF16))
                   for i in range(2)]
            t2b = [p3es.enter_context(nc.sbuf_tensor(f"t2b{i}", [128, 2048], BF16))
                   for i in range(2)]
            t3b = [p3es.enter_context(nc.sbuf_tensor(f"t3b{i}", [128, HHALF * 128], BF16))
                   for i in range(2)]
            mur3 = p3es.enter_context(nc.sbuf_tensor("mur3", [128, 512], F32))
            isr3 = p3es.enter_context(nc.sbuf_tensor("isr3", [128, 512], F32))
            sgb = p3es.enter_context(nc.sbuf_tensor("sgb", [128, 512], F32))
            hfb = p3es.enter_context(nc.sbuf_tensor("hfb", [128, 512], F32))
            xsq3 = [p3es.enter_context(nc.sbuf_tensor(f"xsq3_{i}", [128, 512], F32))
                    for i in range(4)]

            x2v = x2t.ap().rearrange("p (m t) -> p m t", t=512)
            frv = fraw.ap().rearrange("p (m t) -> p m t", t=512)

            t12_pe = [0, 0]
            t3_pe = [0, 0]
            prev_out_v = v_p2
            st_out_prev = 0

            for pp in range(0 if _SKIP_P3 else PAIRS):
                first_w = ([("PE", pe_p2), ("A", a_p2)]
                           if pp == 0 else [])
                for half in range(2):
                    tt = pp * 2 + half
                    for q4 in range(4):
                        E.emit("sync",
                               (lambda tt, half, q4: lambda e: e.dma_start(
                                   out=x2v[:, q4 * 4:(q4 + 1) * 4,
                                           half * 256:(half + 1) * 256],
                                   in_=x2_scr[tt][:, q4 * 4:(q4 + 1) * 4, :]))(tt, half, q4),
                               wait=[("V", prev_out_v), ("STX2", stx2_all),
                                     ("STO", st_out_prev)] + first_w,
                               inc=[("LDX2", 16)])
                ld_x2 = E.n["LDX2"]

                a_s3, _pe = emit_stats_512(
                    E, (lambda i: x2t[:, i * 512:(i + 1) * 512]),
                    mur3, isr3, xsq3, pb[0], pb[1], ones,
                    pe_wait=[("LDX2", ld_x2), ("V", prev_out_v)],
                    src_wait=[("LDX2", ld_x2)])

                for i in range(MC):
                    E.emit("vector",
                           (lambda i: lambda e: e.tensor_sub(
                               hfb[:], x2t[:, i * 512:(i + 1) * 512], mur3[:]))(i),
                           wait=([("A", a_s3)] if i == 0 else []))
                    E.emit("vector",
                           lambda e: e.tensor_mul(hfb[:], hfb[:], isr3[:]))
                    E.emit("vector",
                           (lambda i: lambda e: e.tensor_copy(
                               h2hi[:, i * 512:(i + 1) * 512], hfb[:]))(i))
                    E.emit("vector",
                           (lambda i: lambda e: e.tensor_tensor(
                               h2lo[:, i * 512:(i + 1) * 512], hfb[:],
                               h2hi[:, i * 512:(i + 1) * 512], ALU.subtract))(i),
                           inc=([("V", 1)] if i == MC - 1 else []))
                v_h2 = E.n["V"]

                a_f_marks = []
                v_f_marks = []
                v_hid_marks = []
                for half in range(2):
                    for hl in range(HHALF):
                        hc = half * HHALF + hl
                        buf = hl % 2
                        E.emit("sync",
                               (lambda hc, buf: lambda e: e.dma_start(
                                   out=t1b[buf][:], in_=tw1[hc]))(hc, buf),
                               wait=[("PE", t12_pe[buf]), ("STQ0", stq0_all),
                                     ("STQ1", stq1_all)],
                               inc=[("LDT%d" % buf, 16)])
                        E.emit("sync",
                               (lambda hc, buf: lambda e: e.dma_start(
                                   out=t2b[buf][:], in_=tw2[hc]))(hc, buf),
                               inc=[("LDT%d" % buf, 16)])
                        ld_t = E.n["LDT%d" % buf]
                        bs = 2 + (hl % 2) * 2
                        for i in range(MC):
                            w = [("LDT%d" % buf, ld_t)]
                            if i == 0:
                                w.append(("V", v_h2))
                                if len(v_hid_marks) >= 2:
                                    w.append(("V", v_hid_marks[-2]))
                            E.emit("tensor",
                                   (lambda buf, bs, i: lambda e: e.matmul(
                                       pb[bs][:, 0:512],
                                       t1b[buf][:, i * 128:(i + 1) * 128],
                                       h2hi[:, i * 512:(i + 1) * 512],
                                       start=(i == 0), stop=False))(buf, bs, i),
                                   wait=w)
                            E.emit("tensor",
                                   (lambda buf, bs, i: lambda e: e.matmul(
                                       pb[bs][:, 0:512],
                                       t1b[buf][:, i * 128:(i + 1) * 128],
                                       h2lo[:, i * 512:(i + 1) * 512],
                                       start=False, stop=(i == MC - 1)))(buf, bs, i))
                            E.emit("tensor",
                                   (lambda buf, bs, i: lambda e: e.matmul(
                                       pb[bs + 1][:, 0:512],
                                       t2b[buf][:, i * 128:(i + 1) * 128],
                                       h2hi[:, i * 512:(i + 1) * 512],
                                       start=(i == 0), stop=False))(buf, bs, i))
                            E.emit("tensor",
                                   (lambda buf, bs, i: lambda e: e.matmul(
                                       pb[bs + 1][:, 0:512],
                                       t2b[buf][:, i * 128:(i + 1) * 128],
                                       h2lo[:, i * 512:(i + 1) * 512],
                                       start=False, stop=(i == MC - 1)))(buf, bs, i),
                                   inc=([("PE", 1)] if i == MC - 1 else []))
                        t12_pe[buf] = E.n["PE"]
                        pe_x12 = E.n["PE"]
                        E.emit("scalar",
                               (lambda bs: lambda e: e.activation(
                                   out=sgb[:], in_=pb[bs][:, 0:512],
                                   func=AF.Sigmoid))(bs),
                               wait=[("PE", pe_x12),
                                     ("V", v_hid_marks[-1] if v_hid_marks else 0)],
                               inc=[("A", 1)])
                        a_sg = E.n["A"]
                        E.emit("vector",
                               (lambda bs: lambda e: e.tensor_tensor(
                                   hfb[:], sgb[:], pb[bs + 1][:, 0:512], ALU.mult))(bs),
                               wait=[("A", a_sg)])
                        E.emit("vector",
                               (lambda hl: lambda e: e.tensor_copy(
                                   hidhi[:, hl * 512:(hl + 1) * 512], hfb[:]))(hl))
                        E.emit("vector",
                               (lambda hl: lambda e: e.tensor_tensor(
                                   hidlo[:, hl * 512:(hl + 1) * 512], hfb[:],
                                   hidhi[:, hl * 512:(hl + 1) * 512], ALU.subtract))(hl),
                               inc=[("V", 1)])
                        v_hid_marks.append(E.n["V"])
                    v_hid = E.n["V"]

                    ev_marks = []
                    for oc in range(OC):
                        buf = oc % 2
                        E.emit("sync",
                               (lambda oc, half, buf: lambda e: e.dma_start(
                                   out=t3b[buf][:], in_=tw3[oc, half]))(oc, half, buf),
                               wait=[("PE", t3_pe[buf]), ("STQ0", stq0_all),
                                     ("STQ1", stq1_all)],
                               inc=[("LD3%d" % buf, 16)])
                        ld_t3 = E.n["LD3%d" % buf]
                        bw = 6 + oc % 2
                        for hl in range(HHALF):
                            w = [("LD3%d" % buf, ld_t3)]
                            if hl == 0:
                                w.append(("V", v_hid))
                                if len(ev_marks) >= 2:
                                    w.append(ev_marks[-2])
                            E.emit("tensor",
                                   (lambda buf, bw, hl: lambda e: e.matmul(
                                       pb[bw][:, 0:512],
                                       t3b[buf][:, hl * 128:(hl + 1) * 128],
                                       hidhi[:, hl * 512:(hl + 1) * 512],
                                       start=(hl == 0), stop=False))(buf, bw, hl),
                                   wait=w)
                            E.emit("tensor",
                                   (lambda buf, bw, hl: lambda e: e.matmul(
                                       pb[bw][:, 0:512],
                                       t3b[buf][:, hl * 128:(hl + 1) * 128],
                                       hidlo[:, hl * 512:(hl + 1) * 512],
                                       start=False, stop=(hl == HHALF - 1)))(buf, bw, hl),
                                   inc=([("PE", 1)] if hl == HHALF - 1 else []))
                        t3_pe[buf] = E.n["PE"]
                        pe_w3 = E.n["PE"]
                        if half == 0:
                            E.emit("scalar",
                                   (lambda oc, bw: lambda e: e.copy(
                                       fraw[:, oc * 512:(oc + 1) * 512],
                                       pb[bw][:, 0:512]))(oc, bw),
                                   wait=[("PE", pe_w3)], inc=[("A", 1)])
                            ev_marks.append(("A", E.n["A"]))
                            a_f_marks.append(E.n["A"])
                        else:
                            E.emit("vector",
                                   (lambda oc, bw: lambda e: e.tensor_add(
                                       fraw[:, oc * 512:(oc + 1) * 512],
                                       fraw[:, oc * 512:(oc + 1) * 512],
                                       pb[bw][:, 0:512]))(oc, bw),
                                   wait=[("PE", pe_w3)], inc=[("V", 1)])
                            ev_marks.append(("V", E.n["V"]))
                            v_f_marks.append(E.n["V"])
                a_f = E.n["A"]
                v_f = E.n["V"]

                a_s4, _pe = emit_stats_512(
                    E, (lambda i: fraw[:, i * 512:(i + 1) * 512]),
                    mur3, isr3, xsq3, pb[0], pb[1], ones,
                    pe_wait=[("A", a_f), ("V", v_f)],
                    src_wait=[("A", a_f), ("V", v_f)])
                for i in range(MC):
                    E.emit("vector",
                           (lambda i: lambda e: e.tensor_sub(
                               fraw[:, i * 512:(i + 1) * 512],
                               fraw[:, i * 512:(i + 1) * 512], mur3[:]))(i),
                           wait=([("A", a_s4)] if i == 0 else []))
                    E.emit("vector",
                           (lambda i: lambda e: e.tensor_mul(
                               fraw[:, i * 512:(i + 1) * 512],
                               fraw[:, i * 512:(i + 1) * 512], isr3[:]))(i))
                    E.emit("vector",
                           (lambda i: lambda e: e.tensor_add(
                               fraw[:, i * 512:(i + 1) * 512],
                               fraw[:, i * 512:(i + 1) * 512],
                               x2t[:, i * 512:(i + 1) * 512]))(i),
                           inc=([("V", 1)] if i == MC - 1 else []))
                prev_out_v = E.n["V"]

                for half in range(2):
                    tt = pp * 2 + half
                    for q4 in range(4):
                        E.emit("gpsimd",
                               (lambda tt, half, q4: lambda e: e.dma_start(
                                   out=out[tt][:, q4 * 4:(q4 + 1) * 4, :],
                                   in_=frv[:, q4 * 4:(q4 + 1) * 4,
                                           half * 256:(half + 1) * 256]))(tt, half, q4),
                               wait=[("V", prev_out_v)], inc=[("STO", 16)])
                st_out_prev = E.n["STO"]

            st_final = E.n["STO"]
            E.emit("gpsimd", lambda e: None, wait=[("STO", st_final)])

            E.flush(nc)

    return nc


def emit_stats_512(E, src_fn, mu_t, is_t, xsqb, bk_s, bk_q, ones,
                   pe_wait=(), src_wait=()):
    nb = len(xsqb)
    a_marks = []
    pe_sum0 = E.n["PE"]
    for i in range(MC):
        w = list(src_wait)
        if i >= nb:
            w.append(("PE", pe_sum0 + MC + (i - nb) + 1))
        E.emit("scalar",
               (lambda i: lambda e: e.activation(
                   out=xsqb[i % nb][:], in_=src_fn(i), func=AF.Square))(i),
               wait=w, inc=[("A", 1)])
        a_marks.append(E.n["A"])
    for i in range(MC):
        E.emit("tensor",
               (lambda i: lambda e: e.matmul(
                   bk_s[:, 0:512], ones[:], src_fn(i),
                   start=(i == 0), stop=(i == MC - 1)))(i),
               wait=(list(pe_wait) if i == 0 else []),
               inc=[("PE", 1)])
    for i in range(MC):
        E.emit("tensor",
               (lambda i: lambda e: e.matmul(
                   bk_q[:, 0:512], ones[:], xsqb[i % nb][:],
                   start=(i == 0), stop=(i == MC - 1)))(i),
               wait=[("A", a_marks[i])], inc=[("PE", 1)])
    pe_done = E.n["PE"]
    E.emit("vector", lambda e: e.tensor_scalar_mul(
        mu_t[:], bk_s[:, 0:512], 1.0 / D), wait=[("PE", pe_done)])
    E.emit("vector", lambda e: e.tensor_scalar_mul(
        is_t[:], bk_q[:, 0:512], 1.0 / D))
    E.emit("vector", lambda e: e.tensor_tensor(
        xsqb[0][:], mu_t[:], mu_t[:], ALU.mult))
    E.emit("vector", lambda e: e.tensor_tensor(
        is_t[:], is_t[:], xsqb[0][:], ALU.subtract))
    E.emit("vector", lambda e: e.tensor_scalar_add(is_t[:], is_t[:], LN_EPS))
    E.emit("vector", lambda e: e.reciprocal(is_t[:], is_t[:]), inc=[("V", 1)])
    vr = E.n["V"]
    E.emit("scalar", lambda e: e.activation(
        out=is_t[:], in_=is_t[:], func=AF.Sqrt),
        wait=[("V", vr)], inc=[("A", 1)])
    return E.n["A"], pe_done


_CACHE = {}


def kernel(L0_attn_x, L0_attn_Wq, L0_attn_Wk, L0_attn_Wv, L0_attn_Wo,
           L0_ffn_w1, L0_ffn_w2, L0_ffn_w3):
    x = np.asarray(L0_attn_x, dtype=np.float32)
    Wq = np.asarray(L0_attn_Wq, dtype=np.float32)
    Wk = np.asarray(L0_attn_Wk, dtype=np.float32)
    Wv = np.asarray(L0_attn_Wv, dtype=np.float32)
    Wo = np.asarray(L0_attn_Wo, dtype=np.float32)
    w1 = np.asarray(L0_ffn_w1, dtype=np.float32)
    w2 = np.asarray(L0_ffn_w2, dtype=np.float32)
    w3 = np.asarray(L0_ffn_w3, dtype=np.float32)

    def slabs(wT, ocols):
        return np.ascontiguousarray(
            wT.reshape(MC, 128, ocols // 128, 128).transpose(2, 1, 0, 3))

    wqs = slabs(Wq.T, 2048)
    wks = slabs(Wk.T, 512)
    wos = slabs(Wo.T, 2048)
    wvs = np.ascontiguousarray(Wv.T.reshape(MC, 128, 512).transpose(1, 0, 2))
    w1s = slabs(w1.T, FH)
    w2s = slabs(w2.T, FH)
    w3s = np.ascontiguousarray(
        w3.T.reshape(2, HHALF, 128, OC, 128).transpose(3, 0, 2, 1, 4))

    in_maps = []
    for core in range(8):
        b, seg = core // 4, core % 4
        xf = np.ascontiguousarray(x[b, seg * T:(seg + 1) * T, :].T)
        xsl = np.ascontiguousarray(
            xf.reshape(MC, 128, NT, TT).transpose(2, 1, 0, 3))
        m = np.zeros((128, 8), dtype=np.float32)
        m[:, 0] = float(seg * T)
        for j in range(4):
            m[:, 1 + j] = 1.0 if j < seg else 0.0
        in_maps.append({
            "xs": xsl, "wqs": wqs, "wks": wks, "wvs": wvs, "wos": wos,
            "w1s": w1s, "w2s": w2s, "w3s": w3s, "meta": m,
        })

    if "nc" not in _CACHE:
        _CACHE["nc"] = build()
    nc = _CACHE["nc"]
    res = run_bass_kernel_spmd(nc, in_maps, core_ids=list(range(8)))

    y = np.empty((B, S, D), dtype=np.float32)
    for core in range(8):
        b, seg = core // 4, core % 4
        o = np.asarray(res.results[core]["out"])
        of = o.transpose(2, 1, 0, 3).reshape(D, T)
        y[b, seg * T:(seg + 1) * T, :] = of.T
    return y


# version 09
# speedup vs baseline: 1.0181x; 1.0181x over previous
import contextlib
import numpy as np

import concourse.bass as bass
import concourse.mybir as mybir
from concourse.bass_utils import run_bass_kernel_spmd

F32 = mybir.dt.float32
BF16 = mybir.dt.bfloat16
AF = mybir.ActivationFunctionType
ALU = mybir.AluOpType

B, S, D = 2, 8192, 2048
T = 2048
NT, TT = 8, 256
H, G = 16, 4
MC = D // 128
OC = D // 128
FH = 5632
FHC = FH // 128
HHALF = FHC // 2
NW = FH * D
PAIRS = NT // 2
NCH = T // 128
SW = G * 129

LN_EPS = 1e-5
ATT_EPS = 1e-6
ROPE_BASE = 10000.0

ENGS = ("sync", "gpsimd", "tensor", "vector", "scalar")
DEBUG_SCRATCH = False
_SKIP_CC = False
_SKIP_P3 = False
_SKIP_P2 = False


class Emitter:

    def __init__(self, nc, es):
        self.h = {}
        self.n = {}
        for name in ("LDX", "LDK", "LDS", "LDW0", "LDW1", "LDV0", "LDV1",
                     "LDG0", "LDG1", "LDT0", "LDT1", "LD30", "LD31", "LDX2",
                     "STK", "STX2", "STQ0", "STQ1", "STO", "STC",
                     "PE", "V", "A", "CC", "G"):
            self.h[name] = es.enter_context(nc.semaphore("sem_" + name))
            self.n[name] = 0
        self.ops = {e: [] for e in ENGS}

    def emit(self, eng, fn, wait=(), inc=()):
        for name, k in inc:
            self.n[name] += k
        self.ops[eng].append((fn, tuple(wait), tuple(inc)))

    def flush(self, nc):
        ops = self.ops
        self.ops = {e: [] for e in ENGS}
        h = self.h

        with nc.Block() as block:
            def make_body(eng_name):
                def body(eng):
                    for fn, waits, incs in ops[eng_name]:
                        for name, val in waits:
                            if val > 0:
                                eng.wait_ge(h[name], val)
                        inst = fn(eng)
                        for name, k in incs:
                            inst.then_inc(h[name], k)
                return body

            for eng_name in ENGS:
                getattr(block, eng_name)(make_body(eng_name))


def build():
    nc = bass.Bass("TRN2", target_bir_lowering=False, debug=False,
                   detect_race_conditions=False)

    xs = nc.dram_tensor("xs", [NT, 128, MC, TT], F32, kind="ExternalInput").ap()
    wqs = nc.dram_tensor("wqs", [OC, 128, MC, 128], F32, kind="ExternalInput").ap()
    wks = nc.dram_tensor("wks", [G, 128, MC, 128], F32, kind="ExternalInput").ap()
    wvs = nc.dram_tensor("wvs", [128, MC, 512], F32, kind="ExternalInput").ap()
    wos = nc.dram_tensor("wos", [OC, 128, MC, 128], F32, kind="ExternalInput").ap()
    w1s = nc.dram_tensor("w1s", [FHC, 128, MC, 128], F32, kind="ExternalInput").ap()
    w2s = nc.dram_tensor("w2s", [FHC, 128, MC, 128], F32, kind="ExternalInput").ap()
    w3s = nc.dram_tensor("w3s", [OC, 2, 128, HHALF, 128], F32, kind="ExternalInput").ap()
    meta = nc.dram_tensor("meta", [128, 8], F32, kind="ExternalInput").ap()

    out = nc.dram_tensor("out", [NT, 128, OC, TT], F32, kind="ExternalOutput").ap()

    _skind = "ExternalOutput" if DEBUG_SCRATCH else "Internal"
    kf_scr = nc.dram_tensor("kf_scr", [NT, 128, G, TT], F32, kind=_skind).ap()
    v_scr = nc.dram_tensor("v_scr", [NT, 128, 2, 512], F32, kind=_skind).ap()
    x2_scr = nc.dram_tensor("x2_scr", [NT, 128, MC, TT], F32, kind=_skind).ap()
    st_scr = nc.dram_tensor("st_scr", [NT, 128, 2, TT], F32, kind=_skind).ap()
    tw1 = nc.dram_tensor("tw1", [FHC, 128, MC, 128], BF16, kind=_skind).ap()
    tw2 = nc.dram_tensor("tw2", [FHC, 128, MC, 128], BF16, kind=_skind).ap()
    tw3 = nc.dram_tensor("tw3", [OC, 2, 128, HHALF, 128], BF16, kind=_skind).ap()
    if DEBUG_SCRATCH:
        qf_dbg = nc.dram_tensor("qf_dbg", [NT, 128, H, TT], F32, kind="ExternalOutput").ap()
        oh_dbg = nc.dram_tensor("oh_dbg", [NT, 128, H, TT], F32, kind="ExternalOutput").ap()
        ar_dbg = nc.dram_tensor("ar_dbg", [NT, 128, OC, TT], F32, kind="ExternalOutput").ap()
        tbl_dbg = nc.dram_tensor("tbl_dbg", [64, 2 * T], F32, kind="ExternalOutput").ap()
        thr_dbg = nc.dram_tensor("thr_dbg", [128, 12], F32, kind="ExternalOutput").ap()
        gacc_dbg = nc.dram_tensor("gacc_dbg", [3, 128, 4], F32, kind="ExternalOutput").ap()
    cc_in = nc.dram_tensor("cc_in", [128, SW], F32)
    cc_out = nc.dram_tensor("cc_out", [512, SW], F32)

    pb = [nc.alloc_psum_tensor(f"pb{i}", [128, 512], F32) for i in range(8)]

    with contextlib.ExitStack() as top:
        E = Emitter(nc, top)

        ones = nc.alloc_sbuf_tensor("ones", [128, 128], F32)
        ident = nc.alloc_sbuf_tensor("ident", [128, 128], F32)
        maskge = nc.alloc_sbuf_tensor("maskge", [128, 128], F32)
        meta_sb = nc.alloc_sbuf_tensor("meta_sb", [128, 8], F32)
        invf = nc.alloc_sbuf_tensor("invf", [64, 1], F32)
        iof = nc.alloc_sbuf_tensor("iof", [64, 1], F32)
        thr = nc.alloc_sbuf_tensor("thr", [128, 4], F32)
        nthr = nc.alloc_sbuf_tensor("nthr", [128, 4], F32)
        gacc = nc.alloc_sbuf_tensor("gacc", [128, 4], F32)
        gredbuf = nc.alloc_sbuf_tensor("gredbuf", [128, 128], F32)
        totsb = nc.alloc_sbuf_tensor("totsb", [1, 4], F32)

        def emit_stats(src_fn, width, mu_t, is_t, xsqb, bk_s, bk_q,
                       pe_wait=(), src_wait=()):
            nb = len(xsqb)
            a_marks = []
            pe_sum0 = E.n["PE"]
            for i in range(MC):
                w = list(src_wait)
                if i >= nb:
                    w.append(("PE", pe_sum0 + MC + (i - nb) + 1))
                E.emit("scalar",
                       (lambda i: lambda e: e.activation(
                           out=xsqb[i % nb][:, 0:width], in_=src_fn(i),
                           func=AF.Square))(i),
                       wait=w, inc=[("A", 1)])
                a_marks.append(E.n["A"])
            for i in range(MC):
                E.emit("tensor",
                       (lambda i: lambda e: e.matmul(
                           bk_s[:, 0:width], ones[:], src_fn(i),
                           start=(i == 0), stop=(i == MC - 1)))(i),
                       wait=(list(pe_wait) if i == 0 else []),
                       inc=[("PE", 1)])
            for i in range(MC):
                E.emit("tensor",
                       (lambda i, am: lambda e: e.matmul(
                           bk_q[:, 0:width], ones[:], xsqb[i % nb][:, 0:width],
                           start=(i == 0), stop=(i == MC - 1)))(i, a_marks[i]),
                       wait=[("A", a_marks[i])], inc=[("PE", 1)])
            pe_done = E.n["PE"]
            E.emit("vector", lambda e: e.tensor_scalar_mul(
                mu_t[:, 0:width], bk_s[:, 0:width], 1.0 / D),
                wait=[("PE", pe_done)])
            E.emit("vector", lambda e: e.tensor_scalar_mul(
                is_t[:, 0:width], bk_q[:, 0:width], 1.0 / D))
            E.emit("vector", lambda e: e.tensor_tensor(
                xsqb[0][:, 0:width], mu_t[:, 0:width], mu_t[:, 0:width], ALU.mult))
            E.emit("vector", lambda e: e.tensor_tensor(
                is_t[:, 0:width], is_t[:, 0:width], xsqb[0][:, 0:width],
                ALU.subtract))
            E.emit("vector", lambda e: e.tensor_scalar_add(
                is_t[:, 0:width], is_t[:, 0:width], LN_EPS))
            E.emit("vector", lambda e: e.reciprocal(
                is_t[:, 0:width], is_t[:, 0:width]), inc=[("V", 1)])
            vr = E.n["V"]
            E.emit("scalar", lambda e: e.activation(
                out=is_t[:, 0:width], in_=is_t[:, 0:width], func=AF.Sqrt),
                wait=[("V", vr)], inc=[("A", 1)])
            return E.n["A"], pe_done

        with contextlib.ExitStack() as p12:
            cosT = p12.enter_context(nc.sbuf_tensor("cosT", [64, T], F32))
            sinT = p12.enter_context(nc.sbuf_tensor("sinT", [64, T], F32))
            SS = p12.enter_context(nc.sbuf_tensor("SS", [128, NCH * SW], F32))
            gbuf = [p12.enter_context(nc.sbuf_tensor(f"gbuf{i}", [128, 2816], F32))
                    for i in range(2)]
            qbuf = [p12.enter_context(nc.sbuf_tensor(f"qbuf{i}", [128, 2816], BF16))
                    for i in range(2)]
            m2tmp = p12.enter_context(nc.sbuf_tensor("m2tmp", [128, 2816], BF16))

            with contextlib.ExitStack() as p0tmp:
                thb = p0tmp.enter_context(nc.sbuf_tensor("thb", [64, T], F32))
                r1 = p0tmp.enter_context(nc.sbuf_tensor("r1", [64, T], F32))
                r2 = p0tmp.enter_context(nc.sbuf_tensor("r2", [64, T], F32))
                ri = p0tmp.enter_context(nc.sbuf_tensor("ri", [64, T], mybir.dt.int32))

                E.emit("sync", lambda e: e.dma_start(out=meta_sb[:], in_=meta),
                       inc=[("LDX", 16)])
                ld_meta = E.n["LDX"]

                E.emit("gpsimd", lambda e: e.memset(ones[:], 1.0), inc=[("G", 1)])
                st1 = E.n["G"]
                E.emit("gpsimd", lambda e: e.affine_select(
                    ident[:], ones[:], [[1, 128]], ALU.is_equal, 0.0,
                    base=0, channel_multiplier=-1),
                    wait=[("G", st1)], inc=[("G", 1)])
                E.emit("gpsimd", lambda e: e.affine_select(
                    maskge[:], ones[:], [[1, 128]], ALU.is_ge, 0.0,
                    base=0, channel_multiplier=-1),
                    wait=[("G", st1)], inc=[("G", 1)])
                E.emit("gpsimd", lambda e: e.iota(
                    iof[:], [[0, 1]], channel_multiplier=1,
                    allow_small_or_imprecise_dtypes=True), inc=[("G", 1)])
                st_iof = E.n["G"]
                E.emit("gpsimd", lambda e: e.iota(
                    thb[:], [[1, T]], channel_multiplier=0,
                    allow_small_or_imprecise_dtypes=True), inc=[("G", 1)])
                st_thb = E.n["G"]

                E.emit("scalar", lambda e: e.activation(
                    out=invf[:], in_=iof[:], func=AF.Exp,
                    scale=float(-np.log(ROPE_BASE) / 64.0)),
                    wait=[("G", st_iof)], inc=[("A", 1)])
                a_invf = E.n["A"]

                TWO_PI, PI = float(2 * np.pi), float(np.pi)
                E.emit("vector", lambda e: e.tensor_scalar_add(
                    thb[:], thb[:], meta_sb[0:64, 0:1]),
                    wait=[("LDX", ld_meta), ("G", st_thb), ("A", a_invf)])
                E.emit("vector", lambda e: e.tensor_tensor(
                    thb[:], thb[:], invf[:].broadcast_to((64, T)), ALU.mult))
                E.emit("vector", lambda e: e.tensor_scalar_mul(
                    r1[:], thb[:], float(1.0 / (2 * np.pi))))
                E.emit("vector", lambda e: e.tensor_copy(ri[:], r1[:]))
                E.emit("vector", lambda e: e.tensor_copy(r1[:], ri[:]))
                E.emit("vector", lambda e: e.scalar_tensor_tensor(
                    r1[:], r1[:], -TWO_PI, thb[:], ALU.mult, ALU.add))
                E.emit("vector", lambda e: e.tensor_single_scalar(
                    r2[:], r1[:], PI, ALU.is_gt))
                E.emit("vector", lambda e: e.scalar_tensor_tensor(
                    r1[:], r2[:], -TWO_PI, r1[:], ALU.mult, ALU.add))
                E.emit("vector", lambda e: e.tensor_single_scalar(
                    r2[:], r1[:], -PI, ALU.is_lt))
                E.emit("vector", lambda e: e.scalar_tensor_tensor(
                    r1[:], r2[:], TWO_PI, r1[:], ALU.mult, ALU.add),
                    inc=[("V", 1)])
                v_s = E.n["V"]
                E.emit("scalar", lambda e: e.activation(
                    out=sinT[:], in_=r1[:], func=AF.Sin),
                    wait=[("V", v_s)], inc=[("A", 1)])
                a_sin = E.n["A"]
                E.emit("scalar", lambda e: e.activation(
                    out=r2[:], in_=r1[:], func=AF.Abs),
                    wait=[("V", v_s)], inc=[("A", 1)])
                a_abs = E.n["A"]
                E.emit("vector", lambda e: e.tensor_scalar(
                    r2[:], r2[:], -1.0, float(np.pi / 2), ALU.mult, ALU.add),
                    wait=[("A", a_abs)], inc=[("V", 1)])
                v_c = E.n["V"]
                E.emit("scalar", lambda e: e.activation(
                    out=cosT[:], in_=r2[:], func=AF.Sin),
                    wait=[("V", v_c)], inc=[("A", 1)])
                a_cos = E.n["A"]
                if DEBUG_SCRATCH:
                    E.emit("gpsimd", lambda e: e.dma_start(
                        out=tbl_dbg[:, 0:T], in_=cosT[:]),
                        wait=[("A", a_cos)], inc=[("STO", 16)])
                    E.emit("gpsimd", lambda e: e.dma_start(
                        out=tbl_dbg[:, T:2 * T], in_=sinT[:]),
                        wait=[("A", a_cos)], inc=[("STO", 16)])

                E.flush(nc)

            gamma_chunks = ([(w1s[hc], 2048, 0) for hc in range(FHC)]
                            + [(w2s[hc], 2048, 1) for hc in range(FHC)]
                            + [(w3s[oc, hf], 2816, 2)
                               for oc in range(OC) for hf in range(2)])
            quant_chunks = ([(w1s[hc], tw1[hc], 2048, 0) for hc in range(FHC)]
                            + [(w2s[hc], tw2[hc], 2048, 1) for hc in range(FHC)]
                            + [(w3s[oc, hf], tw3[oc, hf], 2816, 2)
                               for oc in range(OC) for hf in range(2)])

            gst = {"k": 0, "v": [0, 0]}

            def emit_gamma_chunk():
                k = gst["k"]
                if k >= len(gamma_chunks):
                    return
                src, w, wi = gamma_chunks[k]
                buf = k % 2
                E.emit("sync",
                       (lambda src, buf, w: lambda e: e.dma_start(
                           out=gbuf[buf][:, 0:w], in_=src))(src, buf, w),
                       wait=[("V", gst["v"][buf])], inc=[("LDG%d" % buf, 16)])
                ldk = E.n["LDG%d" % buf]
                E.emit("vector",
                       (lambda buf, w, k: lambda e: e.reduce_sum(
                           gredbuf[:, k:k + 1], gbuf[buf][:, 0:w],
                           mybir.AxisListType.X, apply_absolute_value=True))(buf, w, k),
                       wait=[("LDG%d" % buf, ldk)], inc=[("V", 1)])
                gst["v"][buf] = E.n["V"]
                gst["k"] = k + 1
                if DEBUG_SCRATCH and (k + 1) in (FHC, 2 * FHC, 3 * FHC - 12 + 12, len(gamma_chunks)):
                    idx = {FHC: 0, 2 * FHC: 1, len(gamma_chunks): 2}.get(k + 1)
                    if idx is not None:
                        vv = E.n["V"]
                        E.emit("gpsimd",
                               (lambda idx: lambda e: e.dma_start(
                                   out=gacc_dbg[idx], in_=gacc[:]))(idx),
                               wait=[("V", vv)], inc=[("STO", 16)])

            GCOL = [(0, FHC), (FHC, 2 * FHC), (2 * FHC, 2 * FHC + 2 * OC)]

            def emit_thresholds(pe_bank_wait):
                vw = E.n["V"]
                for wi in range(3):
                    lo, hi = GCOL[wi]
                    E.emit("vector",
                           (lambda wi, lo, hi: lambda e: e.reduce_sum(
                               gacc[:, wi:wi + 1], gredbuf[:, lo:hi],
                               mybir.AxisListType.X))(wi, lo, hi),
                           wait=([("V", vw)] if wi == 0 else []),
                           inc=[("V", 1)])
                v_gacc = E.n["V"]
                for wi in range(3):
                    E.emit("tensor",
                           (lambda wi: lambda e: e.matmul(
                               pb[0][0:1, wi:wi + 1], gacc[:, wi:wi + 1],
                               ones[:, 0:1],
                               start=(wi == 0), stop=(wi == 2)))(wi),
                           wait=([("V", v_gacc)] + list(pe_bank_wait))
                           if wi == 0 else [], inc=[("PE", 1)])
                p1_ = E.n["PE"]
                E.emit("vector",
                       lambda e: e.tensor_copy(totsb[0:1, 0:3], pb[0][0:1, 0:3]),
                       wait=[("PE", p1_)], inc=[("V", 1)])
                v1 = E.n["V"]
                E.emit("tensor",
                       lambda e: e.matmul(
                           pb[0][:, 0:3], ones[0:1, :], totsb[0:1, 0:3],
                           start=True, stop=True),
                       wait=[("V", v1)], inc=[("PE", 1)])
                p2_ = E.n["PE"]
                E.emit("vector",
                       lambda e: e.tensor_scalar(
                           thr[:, 0:3], pb[0][:, 0:3],
                           0.5 / NW, 0.5e-8, ALU.mult, ALU.add),
                       wait=[("PE", p2_)])
                E.emit("vector",
                       lambda e: e.tensor_scalar(
                           nthr[:, 0:3], pb[0][:, 0:3],
                           -0.5 / NW, -0.5e-8, ALU.mult, ALU.add),
                       inc=[("V", 1)])
                return E.n["V"]

            qst = {"k": 0, "v": [0, 0], "st": [0, 0], "thr_v": None}

            def emit_quant_chunk():
                k = qst["k"]
                if k >= len(quant_chunks):
                    return
                src, dst, w, wi = quant_chunks[k]
                buf = k % 2
                E.emit("sync",
                       (lambda src, buf, w: lambda e: e.dma_start(
                           out=gbuf[buf][:, 0:w], in_=src))(src, buf, w),
                       wait=[("V", qst["v"][buf])], inc=[("LDG%d" % buf, 16)])
                ldk = E.n["LDG%d" % buf]
                E.emit("vector",
                       (lambda buf, w, wi: lambda e: e.tensor_single_scalar(
                           qbuf[buf][:, 0:w], gbuf[buf][:, 0:w],
                           thr[:, wi:wi + 1], ALU.is_gt))(buf, w, wi),
                       wait=[("LDG%d" % buf, ldk), ("V", qst["thr_v"]),
                             ("STQ%d" % buf, qst["st"][buf])])
                E.emit("vector",
                       (lambda buf, w, wi: lambda e: e.tensor_single_scalar(
                           m2tmp[:, 0:w], gbuf[buf][:, 0:w],
                           nthr[:, wi:wi + 1], ALU.is_lt))(buf, w, wi))
                E.emit("vector",
                       (lambda buf, w: lambda e: e.tensor_tensor(
                           qbuf[buf][:, 0:w], qbuf[buf][:, 0:w],
                           m2tmp[:, 0:w], ALU.subtract))(buf, w),
                       inc=[("V", 1)])
                qst["v"][buf] = E.n["V"]
                E.emit("gpsimd",
                       (lambda dst, buf, w: lambda e: e.dma_start(
                           out=dst, in_=qbuf[buf][:, 0:w]))(dst, buf, w),
                       wait=[("V", qst["v"][buf])], inc=[("STQ%d" % buf, 16)])
                qst["st"][buf] = E.n["STQ%d" % buf]
                qst["k"] = k + 1

            with contextlib.ExitStack() as p1es:
                xt = p1es.enter_context(nc.sbuf_tensor("xt", [128, MC * TT], F32))
                ht = p1es.enter_context(nc.sbuf_tensor("ht", [128, MC * TT], F32))
                qft = p1es.enter_context(nc.sbuf_tensor("qft", [128, H * TT], F32))
                outh = p1es.enter_context(nc.sbuf_tensor("outh", [128, H * TT], F32))
                kft = p1es.enter_context(nc.sbuf_tensor("kft", [128, G * TT], F32))
                vt = p1es.enter_context(nc.sbuf_tensor("vt", [128, 2 * 512], F32))
                wkb = [p1es.enter_context(nc.sbuf_tensor(f"wkb{i}", [128, 2048], F32))
                       for i in range(2)]
                wvb = [p1es.enter_context(nc.sbuf_tensor(f"wvb{i}", [128, 512], F32))
                       for i in range(2)]
                xsqb = [p1es.enter_context(nc.sbuf_tensor(f"xsq{i}", [128, TT], F32))
                        for i in range(8)]
                murep = p1es.enter_context(nc.sbuf_tensor("murep", [128, TT], F32))
                isrep = p1es.enter_context(nc.sbuf_tensor("isrep", [128, TT], F32))
                rt1 = p1es.enter_context(nc.sbuf_tensor("rt1", [64, TT], F32))
                rt2 = p1es.enter_context(nc.sbuf_tensor("rt2", [64, TT], F32))
                t3s = p1es.enter_context(nc.sbuf_tensor("t3s", [128, TT], F32))
                texp = p1es.enter_context(nc.sbuf_tensor("texp", [128, TT], F32))
                kfT = p1es.enter_context(nc.sbuf_tensor("kfT", [128, 128], F32))
                msb = p1es.enter_context(nc.sbuf_tensor("msb", [128, 128], F32))
                dsb = p1es.enter_context(nc.sbuf_tensor("dsb", [128, 128], F32))
                ccg = p1es.enter_context(nc.sbuf_tensor("ccg", [128, 4 * SW], F32))
                spfx = p1es.enter_context(nc.sbuf_tensor("spfx", [128, SW], F32))

                def emit_rope_elu(ps_bank, dst, dcol, tt, pe_w):
                    cs = cosT[:, tt * TT:(tt + 1) * TT]
                    sn = sinT[:, tt * TT:(tt + 1) * TT]
                    E.emit("vector",
                           (lambda ps_bank, cs: lambda e: e.tensor_mul(
                               rt1[:], ps_bank[0:64, 0:TT], cs))(ps_bank, cs),
                           wait=pe_w)
                    E.emit("vector",
                           (lambda ps_bank, sn: lambda e: e.tensor_mul(
                               rt2[:], ps_bank[64:128, 0:TT], sn))(ps_bank, sn))
                    E.emit("vector",
                           (lambda dst, dcol: lambda e: e.tensor_sub(
                               dst[0:64, dcol:dcol + TT], rt1[:], rt2[:]))(dst, dcol))
                    E.emit("vector",
                           (lambda ps_bank, cs: lambda e: e.tensor_mul(
                               rt1[:], ps_bank[64:128, 0:TT], cs))(ps_bank, cs))
                    E.emit("vector",
                           (lambda ps_bank, sn: lambda e: e.tensor_mul(
                               rt2[:], ps_bank[0:64, 0:TT], sn))(ps_bank, sn))
                    E.emit("vector",
                           (lambda dst, dcol: lambda e: e.tensor_add(
                               dst[64:128, dcol:dcol + TT], rt1[:], rt2[:]))(dst, dcol))
                    E.emit("vector",
                           (lambda dst, dcol: lambda e: e.tensor_scalar_min(
                               t3s[:], dst[:, dcol:dcol + TT], 0.0))(dst, dcol),
                           inc=[("V", 1)])
                    vr = E.n["V"]
                    E.emit("scalar",
                           lambda e: e.activation(out=texp[:], in_=t3s[:], func=AF.Exp),
                           wait=[("V", vr)], inc=[("A", 1)])
                    ae = E.n["A"]
                    E.emit("vector",
                           (lambda dst, dcol: lambda e: e.scalar_tensor_tensor(
                               dst[:, dcol:dcol + TT], dst[:, dcol:dcol + TT], 0.0,
                               texp[:], ALU.max, ALU.add))(dst, dcol),
                           wait=[("A", ae)], inc=[("V", 1)])
                    return E.n["V"]

                prev_h = 0
                prev_ss = 0
                st_kf_prev = 0
                st_v_prev = 0
                st_stats_prev = 0
                a_v_prev = 0
                wk_pe = [0, 0]
                wv_pe = [0, 0]

                for tt in range(NT):
                    E.emit("sync",
                           (lambda tt: lambda e: e.dma_start(out=xt[:], in_=xs[tt]))(tt),
                           wait=[("V", prev_h), ("A", a_cos)], inc=[("LDX", 16)])
                    ld_x = E.n["LDX"]

                    a_st, pe_st = emit_stats(
                        (lambda i: xt[:, i * TT:(i + 1) * TT]), TT,
                        murep, isrep, xsqb, pb[0], pb[1],
                        pe_wait=[("LDX", ld_x), ("V", prev_ss),
                                 ("STK", st_stats_prev)],
                        src_wait=[("LDX", ld_x)])
                    E.emit("gpsimd",
                           (lambda tt: lambda e: e.dma_start(
                               out=st_scr[tt][:, 0, :], in_=murep[:]))(tt),
                           wait=[("A", a_st)], inc=[("STK", 16)])
                    E.emit("gpsimd",
                           (lambda tt: lambda e: e.dma_start(
                               out=st_scr[tt][:, 1, :], in_=isrep[:]))(tt),
                           inc=[("STK", 16)])
                    st_stats_prev = E.n["STK"]

                    for i in range(MC):
                        E.emit("vector",
                               (lambda i: lambda e: e.tensor_sub(
                                   ht[:, i * TT:(i + 1) * TT],
                                   xt[:, i * TT:(i + 1) * TT], murep[:]))(i),
                               wait=([("A", a_st)] if i == 0 else []))
                        E.emit("vector",
                               (lambda i: lambda e: e.tensor_mul(
                                   ht[:, i * TT:(i + 1) * TT],
                                   ht[:, i * TT:(i + 1) * TT], isrep[:]))(i),
                               inc=([("V", 1)] if i == MC - 1 else []))
                    prev_h = E.n["V"]
                    v_h = E.n["V"]

                    rope_done = 0
                    for g in range(G):
                        buf = g % 2
                        E.emit("sync",
                               (lambda g, buf: lambda e: e.dma_start(
                                   out=wkb[buf][:], in_=wks[g]))(g, buf),
                               wait=[("PE", wk_pe[buf])], inc=[("LDW%d" % buf, 16)])
                        ld_wk = E.n["LDW%d" % buf]
                        for i in range(MC):
                            w = [("LDW%d" % buf, ld_wk)]
                            if i == 0:
                                w += [("V", v_h), ("V", rope_done),
                                      ("STK", st_kf_prev)]
                            E.emit("tensor",
                                   (lambda g, buf, i: lambda e: e.matmul(
                                       pb[2 + g % 2][:, 0:TT],
                                       wkb[buf][:, i * 128:(i + 1) * 128],
                                       ht[:, i * TT:(i + 1) * TT],
                                       start=(i == 0), stop=(i == MC - 1)))(g, buf, i),
                                   wait=w, inc=([("PE", 1)] if i == MC - 1 else []))
                        wk_pe[buf] = E.n["PE"]
                        rope_done = emit_rope_elu(
                            pb[2 + g % 2], kft, g * TT, tt,
                            [("PE", wk_pe[buf])])
                    v_kf = E.n["V"]

                    for i in range(MC):
                        buf = i % 2
                        E.emit("sync",
                               (lambda i, buf: lambda e: e.dma_start(
                                   out=wvb[buf][:], in_=wvs[:, i, :]))(i, buf),
                               wait=[("PE", wv_pe[buf])], inc=[("LDV%d" % buf, 16)])
                        ld_wv = E.n["LDV%d" % buf]
                        w = [("LDV%d" % buf, ld_wv)]
                        if i == 0:
                            w.append(("A", a_v_prev))
                            w.append(("STK", st_v_prev))
                        E.emit("tensor",
                               (lambda i, buf: lambda e: e.matmul(
                                   pb[4][:, 0:512],
                                   ht[:, i * TT: i * TT + 128],
                                   wvb[buf][:],
                                   start=(i == 0), stop=(i == MC - 1)))(i, buf),
                               wait=w)
                        E.emit("tensor",
                               (lambda i, buf: lambda e: e.matmul(
                                   pb[5][:, 0:512],
                                   ht[:, i * TT + 128: i * TT + 256],
                                   wvb[buf][:],
                                   start=(i == 0), stop=(i == MC - 1)))(i, buf),
                               inc=[("PE", 1)])
                        wv_pe[buf] = E.n["PE"]
                    pe_v = E.n["PE"]
                    for sc in range(2):
                        E.emit("scalar",
                               (lambda sc: lambda e: e.copy(
                                   vt[:, sc * 512:(sc + 1) * 512],
                                   pb[4 + sc][:, 0:512]))(sc),
                               wait=([("PE", pe_v)] if sc == 0 else []),
                               inc=[("A", 1)])
                    a_v = E.n["A"]
                    a_v_prev = a_v

                    E.emit("gpsimd",
                           (lambda tt: lambda e: e.dma_start(
                               out=kf_scr[tt], in_=kft[:]))(tt),
                           wait=[("V", v_kf)], inc=[("STK", 16)])
                    E.emit("gpsimd",
                           (lambda tt: lambda e: e.dma_start(
                               out=v_scr[tt], in_=vt[:]))(tt),
                           wait=[("A", a_v)], inc=[("STK", 16)])
                    st_kf_prev = st_v_prev = E.n["STK"]

                    for c in range(2):
                        gc = tt * 2 + c
                        v_kfT = 0
                        for g in range(G):
                            E.emit("tensor",
                                   (lambda g, c: lambda e: e.transpose(
                                       pb[6][:, 0:128],
                                       kft[:, g * TT + c * 128: g * TT + (c + 1) * 128],
                                       ident[:]))(g, c),
                                   wait=[("V", v_kf), ("V", v_kfT)],
                                   inc=[("PE", 1)])
                            pe_tr = E.n["PE"]
                            E.emit("vector",
                                   lambda e: e.tensor_copy(kfT[:], pb[6][:, 0:128]),
                                   wait=[("PE", pe_tr)], inc=[("V", 1)])
                            v_kfT = E.n["V"]
                            E.emit("tensor",
                                   (lambda g, c: lambda e: e.matmul(
                                       pb[7][:, g * 128:(g + 1) * 128], kfT[:],
                                       vt[:, c * 512 + g * 128: c * 512 + (g + 1) * 128],
                                       start=(g == 0), stop=(g == G - 1)))(g, c),
                                   wait=[("V", v_kfT)]
                                   + ([("V", prev_ss), ("A", a_v)]
                                      if g == 0 else []))
                            E.emit("tensor",
                                   (lambda g: lambda e: e.matmul(
                                       pb[0][:, g:g + 1], kfT[:], ones[:, 0:1],
                                       start=(g == 0), stop=(g == G - 1)))(g),
                                   inc=([("PE", 1)] if g == G - 1 else []))
                        pe_ds = E.n["PE"]
                        base, pbase = gc * SW, (gc - 1) * SW
                        for g in range(G):
                            w = [("PE", pe_ds)] if g == 0 else []
                            if gc == 0:
                                E.emit("vector",
                                       (lambda base, g: lambda e: e.tensor_copy(
                                           SS[:, base + g * 129: base + g * 129 + 128],
                                           pb[7][:, g * 128:(g + 1) * 128]))(base, g),
                                       wait=w)
                                E.emit("vector",
                                       (lambda base, g: lambda e: e.tensor_copy(
                                           SS[:, base + g * 129 + 128: base + g * 129 + 129],
                                           pb[0][:, g:g + 1]))(base, g),
                                       inc=([("V", 1)] if g == G - 1 else []))
                            else:
                                E.emit("vector",
                                       (lambda base, pbase, g: lambda e: e.tensor_add(
                                           SS[:, base + g * 129: base + g * 129 + 128],
                                           SS[:, pbase + g * 129: pbase + g * 129 + 128],
                                           pb[7][:, g * 128:(g + 1) * 128]))(base, pbase, g),
                                       wait=w)
                                E.emit("vector",
                                       (lambda base, pbase, g: lambda e: e.tensor_add(
                                           SS[:, base + g * 129 + 128: base + g * 129 + 129],
                                           SS[:, pbase + g * 129 + 128: pbase + g * 129 + 129],
                                           pb[0][:, g:g + 1]))(base, pbase, g),
                                       inc=([("V", 1)] if g == G - 1 else []))
                        prev_ss = E.n["V"]

                    for _ in range(15):
                        emit_gamma_chunk()

                while gst["k"] < len(gamma_chunks):
                    emit_gamma_chunk()
                qst["thr_v"] = emit_thresholds(pe_bank_wait=[("V", prev_ss)])
                if DEBUG_SCRATCH:
                    def _dump_thr(e):
                        return e.dma_start(out=thr_dbg[:, 0:4], in_=thr[:])
                    E.emit("gpsimd", _dump_thr, wait=[("V", qst["thr_v"])],
                           inc=[("STO", 16)])
                    E.emit("gpsimd", lambda e: e.dma_start(
                        out=thr_dbg[:, 4:8], in_=gacc[:]), inc=[("STO", 16)])
                    E.emit("gpsimd", lambda e: e.dma_start(
                        out=thr_dbg[:, 8:12], in_=nthr[:]), inc=[("STO", 16)])

                quiesce = [(nm, E.n[nm]) for nm in
                           ("PE", "V", "A", "G", "LDX", "LDK", "LDS", "LDW0",
                            "LDW1", "LDV0", "LDV1", "LDG0", "LDG1", "STK",
                            "STX2", "STO", "CC")]
                E.emit("gpsimd",
                       lambda e: e.dma_start(
                           out=cc_in.ap(), in_=SS[:, (NCH - 1) * SW: NCH * SW]),
                       wait=[("V", prev_ss)] + quiesce, inc=[("STC", 16)])
                st_cc = E.n["STC"]
                if not _SKIP_CC:
                    E.emit("gpsimd",
                           lambda e: e.collective_compute(
                               "AllGather", ALU.bypass,
                               replica_groups=[[0, 1, 2, 3], [4, 5, 6, 7]],
                               ins=[cc_in.ap().opt()],
                               outs=[cc_out.ap().opt()]),
                           wait=[("STC", st_cc)], inc=[("CC", 1)])
                cc1 = E.n["CC"]
                for eng in ("sync", "tensor", "vector", "scalar"):
                    E.emit(eng, lambda e: None, wait=[("CC", cc1)])
                for j in range(4):
                    E.emit("gpsimd",
                           (lambda j: lambda e: e.dma_start(
                               out=ccg[:, j * SW:(j + 1) * SW],
                               in_=cc_out.ap()[j * 128:(j + 1) * 128, :]))(j),
                           wait=[("CC", cc1)], inc=[("STC", 16)])
                st_ccg = E.n["STC"]
                E.emit("vector",
                       lambda e: e.tensor_single_scalar(
                           spfx[:], ccg[:, 0:SW], meta_sb[:, 1:2], ALU.mult),
                       wait=[("STC", st_ccg)])
                for j in range(1, 4):
                    E.emit("vector",
                           (lambda j: lambda e: e.scalar_tensor_tensor(
                               spfx[:], ccg[:, j * SW:(j + 1) * SW],
                               meta_sb[:, 1 + j:2 + j], spfx[:],
                               ALU.mult, ALU.add))(j))
                for gc in range(NCH - 1, 0, -1):
                    E.emit("vector",
                           (lambda gc: lambda e: e.tensor_add(
                               SS[:, gc * SW:(gc + 1) * SW],
                               SS[:, (gc - 1) * SW: gc * SW], spfx[:]))(gc))
                E.emit("vector",
                       lambda e: e.tensor_copy(SS[:, 0:SW], spfx[:]),
                       inc=[("V", 1)])
                v_ssready = E.n["V"]

                st_p1 = E.n["STK"]
                pe_p1 = E.n["PE"]
                a_p1 = E.n["A"]

                wq_pe = [0, 0]
                prev_x2v = E.n["V"]
                prev_scan = E.n["V"]
                prev_a_wo = 0
                st_x2_prev = 0

                for tt in range(0 if _SKIP_P2 else NT):
                    first_w = ([("STK", st_p1), ("PE", pe_p1), ("A", a_p1)]
                               if tt == 0 else [])
                    E.emit("sync",
                           (lambda tt: lambda e: e.dma_start(out=xt[:], in_=xs[tt]))(tt),
                           wait=[("V", prev_x2v), ("STX2", st_x2_prev)] + first_w,
                           inc=[("LDX", 16)])
                    ld_x = E.n["LDX"]
                    E.emit("sync",
                           (lambda tt: lambda e: e.dma_start(
                               out=kft[:], in_=kf_scr[tt]))(tt),
                           wait=[("V", prev_scan)] + first_w, inc=[("LDK", 16)])
                    E.emit("sync",
                           (lambda tt: lambda e: e.dma_start(
                               out=vt[:], in_=v_scr[tt]))(tt),
                           inc=[("LDK", 16)])
                    ld_kv = E.n["LDK"]
                    E.emit("sync",
                           (lambda tt: lambda e: e.dma_start(
                               out=murep[:], in_=st_scr[tt][:, 0, :]))(tt),
                           wait=[("V", prev_x2v)] + first_w, inc=[("LDS", 16)])
                    E.emit("sync",
                           (lambda tt: lambda e: e.dma_start(
                               out=isrep[:], in_=st_scr[tt][:, 1, :]))(tt),
                           inc=[("LDS", 16)])
                    ld_st = E.n["LDS"]

                    for i in range(MC):
                        E.emit("vector",
                               (lambda i: lambda e: e.tensor_sub(
                                   ht[:, i * TT:(i + 1) * TT],
                                   xt[:, i * TT:(i + 1) * TT], murep[:]))(i),
                               wait=([("LDX", ld_x), ("LDS", ld_st)]
                                     if i == 0 else []))
                        E.emit("vector",
                               (lambda i: lambda e: e.tensor_mul(
                                   ht[:, i * TT:(i + 1) * TT],
                                   ht[:, i * TT:(i + 1) * TT], isrep[:]))(i),
                               inc=([("V", 1)] if i == MC - 1 else []))
                    v_h = E.n["V"]

                    rope_done = 0
                    for oc in range(H):
                        buf = oc % 2
                        E.emit("sync",
                               (lambda oc, buf: lambda e: e.dma_start(
                                   out=wkb[buf][:], in_=wqs[oc]))(oc, buf),
                               wait=[("PE", wq_pe[buf])], inc=[("LDW%d" % buf, 16)])
                        ld_wq = E.n["LDW%d" % buf]
                        for i in range(MC):
                            w = [("LDW%d" % buf, ld_wq)]
                            if i == 0:
                                w += [("V", v_h), ("V", rope_done),
                                      ("A", prev_a_wo)]
                            E.emit("tensor",
                                   (lambda oc, buf, i: lambda e: e.matmul(
                                       pb[2 + oc % 2][:, 0:TT],
                                       wkb[buf][:, i * 128:(i + 1) * 128],
                                       ht[:, i * TT:(i + 1) * TT],
                                       start=(i == 0), stop=(i == MC - 1)))(oc, buf, i),
                                   wait=w, inc=([("PE", 1)] if i == MC - 1 else []))
                        wq_pe[buf] = E.n["PE"]
                        rope_done = emit_rope_elu(
                            pb[2 + oc % 2], qft, oc * TT, tt,
                            [("PE", wq_pe[buf])])
                    v_qf = E.n["V"]
                    if DEBUG_SCRATCH:
                        E.emit("gpsimd",
                               (lambda tt: lambda e: e.dma_start(
                                   out=qf_dbg[tt], in_=qft[:]))(tt),
                               wait=[("V", v_qf)], inc=[("STO", 16)])
                        dbg_qf_st = E.n["STO"]

                    v_msb = 0
                    v_out = 0
                    for c in range(2):
                        gc = tt * 2 + c
                        sbase = gc * SW
                        for hh in range(H):
                            g = hh // 4
                            qcol = hh * TT + c * 128
                            E.emit("tensor",
                                   (lambda g, c, qcol: lambda e: e.matmul(
                                       pb[4][:, 0:128],
                                       kft[:, g * TT + c * 128: g * TT + (c + 1) * 128],
                                       qft[:, qcol:qcol + 128],
                                       start=True, stop=True))(g, c, qcol),
                                   wait=[("V", v_qf), ("V", v_msb),
                                         ("LDK", ld_kv), ("V", v_ssready)],
                                   inc=[("PE", 1)])
                            pe_a = E.n["PE"]
                            E.emit("vector",
                                   lambda e: e.tensor_mul(
                                       msb[:], pb[4][:, 0:128], maskge[:]),
                                   wait=[("PE", pe_a)], inc=[("V", 1)])
                            v_msb = E.n["V"]
                            E.emit("tensor",
                                   (lambda sbase, g, qcol: lambda e: e.matmul(
                                       pb[5][:, 0:128],
                                       SS[:, sbase + g * 129: sbase + g * 129 + 128],
                                       qft[:, qcol:qcol + 128],
                                       start=True, stop=False))(sbase, g, qcol),
                                   wait=[("V", v_out), ("V", v_msb)])
                            E.emit("tensor",
                                   (lambda c, g: lambda e: e.matmul(
                                       pb[5][:, 0:128],
                                       vt[:, c * 512 + g * 128: c * 512 + (g + 1) * 128],
                                       msb[:], start=False, stop=True))(c, g))
                            E.emit("tensor",
                                   (lambda sbase, g, qcol: lambda e: e.matmul(
                                       pb[6][:, 0:128],
                                       SS[:, sbase + g * 129 + 128: sbase + g * 129 + 129]
                                       .broadcast_to((128, 128)),
                                       qft[:, qcol:qcol + 128],
                                       start=True, stop=False))(sbase, g, qcol))
                            E.emit("tensor",
                                   lambda e: e.matmul(
                                       pb[6][:, 0:128], ones[:], msb[:],
                                       start=False, stop=True),
                                   inc=[("PE", 1)])
                            pe_nd = E.n["PE"]
                            E.emit("vector",
                                   lambda e: e.tensor_scalar_add(
                                       dsb[:], pb[6][:, 0:128], ATT_EPS),
                                   wait=[("PE", pe_nd)])
                            E.emit("vector", lambda e: e.reciprocal(dsb[:], dsb[:]))
                            E.emit("vector",
                                   (lambda hh, c: lambda e: e.tensor_mul(
                                       outh[:, hh * TT + c * 128: hh * TT + (c + 1) * 128],
                                       pb[5][:, 0:128], dsb[:]))(hh, c),
                                   inc=[("V", 1)])
                            v_out = E.n["V"]
                    v_scan = E.n["V"]
                    prev_scan = v_scan
                    if DEBUG_SCRATCH:
                        E.emit("gpsimd",
                               (lambda tt: lambda e: e.dma_start(
                                   out=oh_dbg[tt], in_=outh[:]))(tt),
                               wait=[("V", v_scan)], inc=[("STO", 16)])

                    araw = qft
                    a_wo_marks = []
                    for oc in range(OC):
                        buf = oc % 2
                        E.emit("sync",
                               (lambda oc, buf: lambda e: e.dma_start(
                                   out=wkb[buf][:], in_=wos[oc]))(oc, buf),
                               wait=[("PE", wq_pe[buf])], inc=[("LDW%d" % buf, 16)])
                        ld_wo = E.n["LDW%d" % buf]
                        for i in range(MC):
                            w = [("LDW%d" % buf, ld_wo)]
                            if i == 0:
                                w.append(("V", v_scan))
                                if DEBUG_SCRATCH:
                                    w.append(("STO", dbg_qf_st))
                                if len(a_wo_marks) >= 2:
                                    w.append(("A", a_wo_marks[-2]))
                            E.emit("tensor",
                                   (lambda oc, buf, i: lambda e: e.matmul(
                                       pb[2 + oc % 2][:, 0:TT],
                                       wkb[buf][:, i * 128:(i + 1) * 128],
                                       outh[:, i * TT:(i + 1) * TT],
                                       start=(i == 0), stop=(i == MC - 1)))(oc, buf, i),
                                   wait=w, inc=([("PE", 1)] if i == MC - 1 else []))
                        wq_pe[buf] = E.n["PE"]
                        pe_wo = E.n["PE"]
                        E.emit("scalar",
                               (lambda oc: lambda e: e.copy(
                                   araw[:, oc * TT:(oc + 1) * TT],
                                   pb[2 + oc % 2][:, 0:TT]))(oc),
                               wait=[("PE", pe_wo)], inc=[("A", 1)])
                        a_wo_marks.append(E.n["A"])
                    prev_a_wo = E.n["A"]

                    if DEBUG_SCRATCH:
                        E.emit("gpsimd",
                               (lambda tt: lambda e: e.dma_start(
                                   out=ar_dbg[tt], in_=araw[:]))(tt),
                               wait=[("A", prev_a_wo)], inc=[("STO", 16)])

                    a_st2, _ = emit_stats(
                        (lambda i: araw[:, i * TT:(i + 1) * TT]), TT,
                        murep, isrep, xsqb, pb[0], pb[1],
                        pe_wait=[("A", prev_a_wo)],
                        src_wait=[("A", prev_a_wo)])
                    for i in range(MC):
                        E.emit("vector",
                               (lambda i: lambda e: e.tensor_sub(
                                   t3s[:], araw[:, i * TT:(i + 1) * TT], murep[:]))(i),
                               wait=([("A", a_st2)] if i == 0 else []))
                        E.emit("vector",
                               lambda e: e.tensor_mul(t3s[:], t3s[:], isrep[:]))
                        E.emit("vector",
                               (lambda i: lambda e: e.tensor_add(
                                   xt[:, i * TT:(i + 1) * TT],
                                   xt[:, i * TT:(i + 1) * TT], t3s[:]))(i),
                               inc=([("V", 1)] if i == MC - 1 else []))
                    prev_x2v = E.n["V"]
                    E.emit("gpsimd",
                           (lambda tt: lambda e: e.dma_start(
                               out=x2_scr[tt], in_=xt[:]))(tt),
                           wait=[("V", prev_x2v), ("STX2", st_x2_prev)],
                           inc=[("STX2", 16)])
                    st_x2_prev = E.n["STX2"]

                    for _ in range(15):
                        emit_quant_chunk()

                while qst["k"] < len(quant_chunks):
                    emit_quant_chunk()

                stq0_all = E.n["STQ0"]
                stq1_all = E.n["STQ1"]
                stx2_all = E.n["STX2"]
                pe_p2 = E.n["PE"]
                a_p2 = E.n["A"]
                v_p2 = E.n["V"]

                E.flush(nc)

        with contextlib.ExitStack() as p3es:
            x2t = p3es.enter_context(nc.sbuf_tensor("x2t", [128, MC * 512], F32))
            h2hi = p3es.enter_context(nc.sbuf_tensor("h2hi", [128, MC * 512], BF16))
            h2lo = p3es.enter_context(nc.sbuf_tensor("h2lo", [128, MC * 512], BF16))
            hidhi = p3es.enter_context(nc.sbuf_tensor("hidhi", [128, HHALF * 512], BF16))
            hidlo = p3es.enter_context(nc.sbuf_tensor("hidlo", [128, HHALF * 512], BF16))
            fraw = p3es.enter_context(nc.sbuf_tensor("fraw", [128, OC * 512], F32))
            t1b = [p3es.enter_context(nc.sbuf_tensor(f"t1b{i}", [128, 2048], BF16))
                   for i in range(2)]
            t2b = [p3es.enter_context(nc.sbuf_tensor(f"t2b{i}", [128, 2048], BF16))
                   for i in range(2)]
            t3b = [p3es.enter_context(nc.sbuf_tensor(f"t3b{i}", [128, HHALF * 128], BF16))
                   for i in range(2)]
            mur3 = p3es.enter_context(nc.sbuf_tensor("mur3", [128, 512], F32))
            isr3 = p3es.enter_context(nc.sbuf_tensor("isr3", [128, 512], F32))
            sgb = p3es.enter_context(nc.sbuf_tensor("sgb", [128, 512], F32))
            hfb = p3es.enter_context(nc.sbuf_tensor("hfb", [128, 512], F32))
            xsq3 = [p3es.enter_context(nc.sbuf_tensor(f"xsq3_{i}", [128, 512], F32))
                    for i in range(4)]

            x2v = x2t.ap().rearrange("p (m t) -> p m t", t=512)
            frv = fraw.ap().rearrange("p (m t) -> p m t", t=512)

            t12_pe = [0, 0]
            t3_pe = [0, 0]
            prev_out_v = v_p2
            st_out_prev = 0

            for pp in range(0 if _SKIP_P3 else PAIRS):
                first_w = ([("PE", pe_p2), ("A", a_p2)]
                           if pp == 0 else [])
                for half in range(2):
                    tt = pp * 2 + half
                    for q4 in range(4):
                        E.emit("sync",
                               (lambda tt, half, q4: lambda e: e.dma_start(
                                   out=x2v[:, q4 * 4:(q4 + 1) * 4,
                                           half * 256:(half + 1) * 256],
                                   in_=x2_scr[tt][:, q4 * 4:(q4 + 1) * 4, :]))(tt, half, q4),
                               wait=[("V", prev_out_v), ("STX2", stx2_all),
                                     ("STO", st_out_prev)] + first_w,
                               inc=[("LDX2", 16)])
                ld_x2 = E.n["LDX2"]

                a_s3, _pe = emit_stats_512(
                    E, (lambda i: x2t[:, i * 512:(i + 1) * 512]),
                    mur3, isr3, xsq3, pb[0], pb[1], ones,
                    pe_wait=[("LDX2", ld_x2), ("V", prev_out_v)],
                    src_wait=[("LDX2", ld_x2)])

                for i in range(MC):
                    E.emit("vector",
                           (lambda i: lambda e: e.tensor_sub(
                               hfb[:], x2t[:, i * 512:(i + 1) * 512], mur3[:]))(i),
                           wait=([("A", a_s3)] if i == 0 else []))
                    E.emit("vector",
                           lambda e: e.tensor_mul(hfb[:], hfb[:], isr3[:]))
                    E.emit("vector",
                           (lambda i: lambda e: e.tensor_copy(
                               h2hi[:, i * 512:(i + 1) * 512], hfb[:]))(i))
                    E.emit("vector",
                           (lambda i: lambda e: e.tensor_tensor(
                               h2lo[:, i * 512:(i + 1) * 512], hfb[:],
                               h2hi[:, i * 512:(i + 1) * 512], ALU.subtract))(i),
                           inc=([("V", 1)] if i == MC - 1 else []))
                v_h2 = E.n["V"]

                a_f_marks = []
                v_f_marks = []
                v_hid_marks = []
                for half in range(2):
                    for hl in range(HHALF):
                        hc = half * HHALF + hl
                        buf = hl % 2
                        E.emit("sync",
                               (lambda hc, buf: lambda e: e.dma_start(
                                   out=t1b[buf][:], in_=tw1[hc]))(hc, buf),
                               wait=[("PE", t12_pe[buf]), ("STQ0", stq0_all),
                                     ("STQ1", stq1_all)],
                               inc=[("LDT%d" % buf, 16)])
                        E.emit("sync",
                               (lambda hc, buf: lambda e: e.dma_start(
                                   out=t2b[buf][:], in_=tw2[hc]))(hc, buf),
                               inc=[("LDT%d" % buf, 16)])
                        ld_t = E.n["LDT%d" % buf]
                        bs = 2 + (hl % 2) * 2
                        for i in range(MC):
                            w = [("LDT%d" % buf, ld_t)]
                            if i == 0:
                                w.append(("V", v_h2))
                                if len(v_hid_marks) >= 2:
                                    w.append(("V", v_hid_marks[-2]))
                            E.emit("tensor",
                                   (lambda buf, bs, i: lambda e: e.matmul(
                                       pb[bs][:, 0:512],
                                       t1b[buf][:, i * 128:(i + 1) * 128],
                                       h2hi[:, i * 512:(i + 1) * 512],
                                       start=(i == 0), stop=False))(buf, bs, i),
                                   wait=w)
                            E.emit("tensor",
                                   (lambda buf, bs, i: lambda e: e.matmul(
                                       pb[bs][:, 0:512],
                                       t1b[buf][:, i * 128:(i + 1) * 128],
                                       h2lo[:, i * 512:(i + 1) * 512],
                                       start=False, stop=(i == MC - 1)))(buf, bs, i))
                            E.emit("tensor",
                                   (lambda buf, bs, i: lambda e: e.matmul(
                                       pb[bs + 1][:, 0:512],
                                       t2b[buf][:, i * 128:(i + 1) * 128],
                                       h2hi[:, i * 512:(i + 1) * 512],
                                       start=(i == 0), stop=False))(buf, bs, i))
                            E.emit("tensor",
                                   (lambda buf, bs, i: lambda e: e.matmul(
                                       pb[bs + 1][:, 0:512],
                                       t2b[buf][:, i * 128:(i + 1) * 128],
                                       h2lo[:, i * 512:(i + 1) * 512],
                                       start=False, stop=(i == MC - 1)))(buf, bs, i),
                                   inc=([("PE", 1)] if i == MC - 1 else []))
                        t12_pe[buf] = E.n["PE"]
                        pe_x12 = E.n["PE"]
                        E.emit("scalar",
                               (lambda bs: lambda e: e.activation(
                                   out=sgb[:], in_=pb[bs][:, 0:512],
                                   func=AF.Sigmoid))(bs),
                               wait=[("PE", pe_x12),
                                     ("V", v_hid_marks[-1] if v_hid_marks else 0)],
                               inc=[("A", 1)])
                        a_sg = E.n["A"]
                        E.emit("vector",
                               (lambda bs: lambda e: e.tensor_tensor(
                                   hfb[:], sgb[:], pb[bs + 1][:, 0:512], ALU.mult))(bs),
                               wait=[("A", a_sg)])
                        E.emit("vector",
                               (lambda hl: lambda e: e.tensor_copy(
                                   hidhi[:, hl * 512:(hl + 1) * 512], hfb[:]))(hl))
                        E.emit("vector",
                               (lambda hl: lambda e: e.tensor_tensor(
                                   hidlo[:, hl * 512:(hl + 1) * 512], hfb[:],
                                   hidhi[:, hl * 512:(hl + 1) * 512], ALU.subtract))(hl),
                               inc=[("V", 1)])
                        v_hid_marks.append(E.n["V"])
                    v_hid = E.n["V"]

                    ev_marks = []
                    for oc in range(OC):
                        buf = oc % 2
                        E.emit("sync",
                               (lambda oc, half, buf: lambda e: e.dma_start(
                                   out=t3b[buf][:], in_=tw3[oc, half]))(oc, half, buf),
                               wait=[("PE", t3_pe[buf]), ("STQ0", stq0_all),
                                     ("STQ1", stq1_all)],
                               inc=[("LD3%d" % buf, 16)])
                        ld_t3 = E.n["LD3%d" % buf]
                        bw = 6 + oc % 2
                        for hl in range(HHALF):
                            w = [("LD3%d" % buf, ld_t3)]
                            if hl == 0:
                                w.append(("V", v_hid))
                                if len(ev_marks) >= 2:
                                    w.append(ev_marks[-2])
                            E.emit("tensor",
                                   (lambda buf, bw, hl: lambda e: e.matmul(
                                       pb[bw][:, 0:512],
                                       t3b[buf][:, hl * 128:(hl + 1) * 128],
                                       hidhi[:, hl * 512:(hl + 1) * 512],
                                       start=(hl == 0), stop=False))(buf, bw, hl),
                                   wait=w)
                            E.emit("tensor",
                                   (lambda buf, bw, hl: lambda e: e.matmul(
                                       pb[bw][:, 0:512],
                                       t3b[buf][:, hl * 128:(hl + 1) * 128],
                                       hidlo[:, hl * 512:(hl + 1) * 512],
                                       start=False, stop=(hl == HHALF - 1)))(buf, bw, hl),
                                   inc=([("PE", 1)] if hl == HHALF - 1 else []))
                        t3_pe[buf] = E.n["PE"]
                        pe_w3 = E.n["PE"]
                        if half == 0:
                            E.emit("scalar",
                                   (lambda oc, bw: lambda e: e.copy(
                                       fraw[:, oc * 512:(oc + 1) * 512],
                                       pb[bw][:, 0:512]))(oc, bw),
                                   wait=[("PE", pe_w3)], inc=[("A", 1)])
                            ev_marks.append(("A", E.n["A"]))
                            a_f_marks.append(E.n["A"])
                        else:
                            E.emit("vector",
                                   (lambda oc, bw: lambda e: e.tensor_add(
                                       fraw[:, oc * 512:(oc + 1) * 512],
                                       fraw[:, oc * 512:(oc + 1) * 512],
                                       pb[bw][:, 0:512]))(oc, bw),
                                   wait=[("PE", pe_w3)], inc=[("V", 1)])
                            ev_marks.append(("V", E.n["V"]))
                            v_f_marks.append(E.n["V"])
                a_f = E.n["A"]
                v_f = E.n["V"]

                a_s4, _pe = emit_stats_512(
                    E, (lambda i: fraw[:, i * 512:(i + 1) * 512]),
                    mur3, isr3, xsq3, pb[0], pb[1], ones,
                    pe_wait=[("A", a_f), ("V", v_f)],
                    src_wait=[("A", a_f), ("V", v_f)])
                for i in range(MC):
                    E.emit("vector",
                           (lambda i: lambda e: e.tensor_sub(
                               fraw[:, i * 512:(i + 1) * 512],
                               fraw[:, i * 512:(i + 1) * 512], mur3[:]))(i),
                           wait=([("A", a_s4)] if i == 0 else []))
                    E.emit("vector",
                           (lambda i: lambda e: e.tensor_mul(
                               fraw[:, i * 512:(i + 1) * 512],
                               fraw[:, i * 512:(i + 1) * 512], isr3[:]))(i))
                    E.emit("vector",
                           (lambda i: lambda e: e.tensor_add(
                               fraw[:, i * 512:(i + 1) * 512],
                               fraw[:, i * 512:(i + 1) * 512],
                               x2t[:, i * 512:(i + 1) * 512]))(i),
                           inc=([("V", 1)] if i == MC - 1 else []))
                prev_out_v = E.n["V"]

                for half in range(2):
                    tt = pp * 2 + half
                    for q4 in range(4):
                        E.emit("gpsimd",
                               (lambda tt, half, q4: lambda e: e.dma_start(
                                   out=out[tt][:, q4 * 4:(q4 + 1) * 4, :],
                                   in_=frv[:, q4 * 4:(q4 + 1) * 4,
                                           half * 256:(half + 1) * 256]))(tt, half, q4),
                               wait=[("V", prev_out_v)], inc=[("STO", 16)])
                st_out_prev = E.n["STO"]

            st_final = E.n["STO"]
            E.emit("gpsimd", lambda e: None, wait=[("STO", st_final)])

            E.flush(nc)

    return nc


def emit_stats_512(E, src_fn, mu_t, is_t, xsqb, bk_s, bk_q, ones,
                   pe_wait=(), src_wait=()):
    nb = len(xsqb)
    a_marks = []
    pe_sum0 = E.n["PE"]
    for i in range(MC):
        w = list(src_wait)
        if i >= nb:
            w.append(("PE", pe_sum0 + MC + (i - nb) + 1))
        E.emit("scalar",
               (lambda i: lambda e: e.activation(
                   out=xsqb[i % nb][:], in_=src_fn(i), func=AF.Square))(i),
               wait=w, inc=[("A", 1)])
        a_marks.append(E.n["A"])
    for i in range(MC):
        E.emit("tensor",
               (lambda i: lambda e: e.matmul(
                   bk_s[:, 0:512], ones[:], src_fn(i),
                   start=(i == 0), stop=(i == MC - 1)))(i),
               wait=(list(pe_wait) if i == 0 else []),
               inc=[("PE", 1)])
    for i in range(MC):
        E.emit("tensor",
               (lambda i: lambda e: e.matmul(
                   bk_q[:, 0:512], ones[:], xsqb[i % nb][:],
                   start=(i == 0), stop=(i == MC - 1)))(i),
               wait=[("A", a_marks[i])], inc=[("PE", 1)])
    pe_done = E.n["PE"]
    E.emit("vector", lambda e: e.tensor_scalar_mul(
        mu_t[:], bk_s[:, 0:512], 1.0 / D), wait=[("PE", pe_done)])
    E.emit("vector", lambda e: e.tensor_scalar_mul(
        is_t[:], bk_q[:, 0:512], 1.0 / D))
    E.emit("vector", lambda e: e.tensor_tensor(
        xsqb[0][:], mu_t[:], mu_t[:], ALU.mult))
    E.emit("vector", lambda e: e.tensor_tensor(
        is_t[:], is_t[:], xsqb[0][:], ALU.subtract))
    E.emit("vector", lambda e: e.tensor_scalar_add(is_t[:], is_t[:], LN_EPS))
    E.emit("vector", lambda e: e.reciprocal(is_t[:], is_t[:]), inc=[("V", 1)])
    vr = E.n["V"]
    E.emit("scalar", lambda e: e.activation(
        out=is_t[:], in_=is_t[:], func=AF.Sqrt),
        wait=[("V", vr)], inc=[("A", 1)])
    return E.n["A"], pe_done


_CACHE = {}


def kernel(L0_attn_x, L0_attn_Wq, L0_attn_Wk, L0_attn_Wv, L0_attn_Wo,
           L0_ffn_w1, L0_ffn_w2, L0_ffn_w3):
    x = np.asarray(L0_attn_x, dtype=np.float32)
    Wq = np.asarray(L0_attn_Wq, dtype=np.float32)
    Wk = np.asarray(L0_attn_Wk, dtype=np.float32)
    Wv = np.asarray(L0_attn_Wv, dtype=np.float32)
    Wo = np.asarray(L0_attn_Wo, dtype=np.float32)
    w1 = np.asarray(L0_ffn_w1, dtype=np.float32)
    w2 = np.asarray(L0_ffn_w2, dtype=np.float32)
    w3 = np.asarray(L0_ffn_w3, dtype=np.float32)

    def slabs(wT, ocols):
        return np.ascontiguousarray(
            wT.reshape(MC, 128, ocols // 128, 128).transpose(2, 1, 0, 3))

    wqs = slabs(Wq.T, 2048)
    wks = slabs(Wk.T, 512)
    wos = slabs(Wo.T, 2048)
    wvs = np.ascontiguousarray(Wv.T.reshape(MC, 128, 512).transpose(1, 0, 2))
    w1s = slabs(w1.T, FH)
    w2s = slabs(w2.T, FH)
    w3s = np.ascontiguousarray(
        w3.T.reshape(2, HHALF, 128, OC, 128).transpose(3, 0, 2, 1, 4))

    in_maps = []
    for core in range(8):
        b, seg = core // 4, core % 4
        xf = np.ascontiguousarray(x[b, seg * T:(seg + 1) * T, :].T)
        xsl = np.ascontiguousarray(
            xf.reshape(MC, 128, NT, TT).transpose(2, 1, 0, 3))
        m = np.zeros((128, 8), dtype=np.float32)
        m[:, 0] = float(seg * T)
        for j in range(4):
            m[:, 1 + j] = 1.0 if j < seg else 0.0
        in_maps.append({
            "xs": xsl, "wqs": wqs, "wks": wks, "wvs": wvs, "wos": wos,
            "w1s": w1s, "w2s": w2s, "w3s": w3s, "meta": m,
        })

    if "nc" not in _CACHE:
        _CACHE["nc"] = build()
    nc = _CACHE["nc"]
    res = run_bass_kernel_spmd(nc, in_maps, core_ids=list(range(8)))

    y = np.empty((B, S, D), dtype=np.float32)
    for core in range(8):
        b, seg = core // 4, core % 4
        o = np.asarray(res.results[core]["out"])
        of = o.transpose(2, 1, 0, 3).reshape(D, T)
        y[b, seg * T:(seg + 1) * T, :] = of.T
    return y
